# version 02
# baseline (speedup 1.0000x reference)
import contextlib
import math
import numpy as np
import concourse.bass as bass
import concourse.mybir as mybir
from concourse.bass_utils import run_bass_kernel_spmd

F32 = mybir.dt.float32
BF16 = mybir.dt.bfloat16
I32 = mybir.dt.int32
AF = mybir.ActivationFunctionType
ALU = mybir.AluOpType
AX = mybir.AxisListType

D, DC, L, TOK = 2048, 16, 2048, 1024
NH, NKV, HD = 16, 4, 128
EPS = 1e-5
NEG = -1.0e30
SCL = 1.0 / math.sqrt(HD)
AE, AI, AIC = 8, 1024, 8
FE, FI, FIC = 16, 1408, 11
SI, SIC = 2816, 22

ENGS = ("sync", "vector", "scalar", "tensor", "gpsimd")


class Sem:
    __slots__ = ("h", "name", "producer", "dma", "count")

    def __init__(self, h, name, producer, dma=False):
        self.h, self.name, self.producer, self.dma, self.count = h, name, producer, dma, 0


class Prog:
    def __init__(self):
        self.ops = {e: [] for e in ENGS}

    def emit(self, eng, fn, waits=(), inc=None):
        rec = [list(waits), fn, inc]
        if inc is not None:
            inc[0].count += inc[1]
        self.ops[eng].append(rec)
        return rec

    def ms(self, eng, sem):
        ops = self.ops[eng]
        assert ops, f"no ops on {eng}"
        rec = ops[-1]
        if rec[2] is None:
            sem.count += 1
            rec[2] = (sem, 1)
        s = rec[2][0]
        return (s, s.count)

    def replay(self, block):
        def mk(name):
            ops = self.ops[name]

            def body(eng):
                lw = {}
                for waits, fn, inc in ops:
                    for (sem, val) in waits:
                        if lw.get(sem.name, -1) >= val:
                            continue
                        eng.wait_ge(sem.h, val)
                        lw[sem.name] = val
                    inst = fn(eng)
                    if inc is not None:
                        inst.then_inc(inc[0].h, inc[1])
            return body

        block.sync(mk("sync"))
        block.vector(mk("vector"))
        block.scalar(mk("scalar"))
        block.tensor(mk("tensor"))
        block.gpsimd(mk("gpsimd"))


class DSlot:

    def __init__(self, P, sem):
        self.P, self.sem, self.fills, self.guard = P, sem, 0, []

    def dma(self, dst, src):
        w = list(self.guard)
        self.P.emit("gpsimd", lambda e, d=dst, s=src: e.dma_start(d, s), waits=w,
                    inc=(self.sem, 16))
        self.fills += 1
        self.guard = []
        return (self.sem, 16 * self.fills)


def build_program(dbg=(), norope=0):
    nc = bass.Bass("TRN2", target_bir_lowering=False, detect_race_conditions=False)
    P = Prog()
    es = contextlib.ExitStack()


    def kcm(dap, off, k, ncols, rowlen):
        return bass.AP(dap.tensor, dap.offset + off,
                       [[rowlen, 128], [128 * rowlen, k], [1, ncols]])

    def din(name, shape, dtype=F32):
        return nc.dram_tensor(name, shape, dtype, kind="ExternalInput").ap()

    xin = din("xin", [D, L])
    wq_d = din("wq", [D, 2 * D])
    wk_d = din("wk", [D, 2 * NKV * HD])
    wv_d = din("wv", [D, NKV * HD])
    wo_d = din("wo", [D, D])
    anw_d = din("anw", [128, DC])
    mnw_d = din("mnw", [128, DC])
    gwa_d = din("gwa", [D, AE])
    w13a_d = din("w13a", [AE, D, 2 * AI])
    w2a_d = din("w2a", [AE, AI, D])
    gwf_d = din("gwf", [D, FE])
    w1f_d = din("w1f", [FE, D, 2 * FI])
    w2f_d = din("w2f", [FE, FI, D])
    sw1_d = din("sw1", [D, 2 * SI])
    sw2_d = din("sw2", [SI, D])
    posr_d = din("posr", [1, L])
    pmask_d = din("pmask", [128, 128])
    out = nc.dram_tensor("out", [D, TOK], F32, kind="ExternalOutput").ap()

    dbg_t = {}
    for name, shp in dict(
        xn=[D, TOK], q=[D, TOK], kf=[1024, 1024], vt=[1024, 1024],
        ao=[D, TOK], acc1=[D, TOK], hm=[D, TOK], accA=[D, TOK], mbr=[NH, TOK],
        tab=[128, 2048], swp=[128, 128], qs=[128, 512], swe=[128, 384],
        p1s=[128, 1024], p1t=[128, 16], p2e=[128, 1024],
        sh=[D, TOK], accF=[D, TOK],
    ).items():
        if name in dbg:
            dbg_t[name] = nc.dram_tensor("dbg_" + name, shp, F32, kind="ExternalOutput").ap()

    def SB(name, shape, dtype):
        return es.enter_context(nc.sbuf_tensor(name, shape, dtype))

    acc = SB("acc", [128, DC * TOK], F32)
    accB = acc.bitcast(BF16)
    xnb = SB("xnb", [128, DC * TOK], BF16)
    AR = SB("AR", [128, 49152], BF16)
    ARf = AR.bitcast(F32)
    STT = SB("STT", [128, 48], F32)
    GL = SB("GL", [128, 2048], F32)
    GTM = SB("GTM", [128, 256], F32)
    GF = SB("GF", [128, 512], F32)
    GFB = GF.bitcast(BF16)
    PM = SB("PM", [128, 128], F32)
    CT = SB("CT", [128, 512], F32)
    CTB = SB("CTB", [128, 16], BF16)
    CBC = GL[:, 1024:2048]
    ANW = SB("ANW", [128, DC], F32)
    MNW = SB("MNW", [128, DC], F32)
    GW = SB("GW", [128, DC * (AE + FE)], BF16)
    ITAB = GF.bitcast(I32)
    IOT = GF[:, 128:256]
    ONER = SB("ONER", [1, 128], F32)
    IFD = SB("IFD", [1, 128], F32)
    psb = [es.enter_context(nc.psum_tensor(f"ps{i}", [128, 512], F32)) for i in range(8)]

    IDENT = CT[:, 0:128]
    TRIL = CT[:, 128:256]
    TRIU = CT[:, 256:384]
    C_NEGHALF = CT[:, 384:385]
    C_EPS = CT[:, 385:386]
    C_INVD = CT[:, 386:387]
    C_LOGB = CT[:, 387:388]
    C_I2PI = CT[:, 388:389]
    C_N2PI = CT[:, 389:390]
    C_P2PI = CT[:, 393:394]
    C_PI = CT[:, 394:395]
    C_NPI = CT[:, 395:396]
    C_HALFPI = CT[:, 390:391]
    C_NEG1 = CT[:, 391:392]
    C_SCL = CT[:, 392:393]
    ONEC = CTB[:, 0:1]

    def sem(name, producer, dma=False):
        return Sem(es.enter_context(nc.semaphore(name)), name, producer, dma)

    s_t = sem("s_t", "tensor")
    s_v = sem("s_v", "vector")
    s_sc = sem("s_sc", "scalar")
    s_gp = sem("s_gp", "gpsimd")
    s_x = [sem(f"s_x{i}", "dma", True) for i in range(2)]
    s_cst = sem("s_cst", "dma", True)
    s_out = sem("s_out", "dma", True)
    dslot = [DSlot(P, sem(f"s_wr{i}", "dma", True)) for i in range(8)]

    ps_guard = [[] for _ in range(8)]

    def mm(ap, lhsT, rhs, start, stop, waits=(), bank=None):
        w = list(waits)
        if bank is not None and start:
            w = w + ps_guard[bank]
        P.emit("tensor", lambda e, a=ap, l=lhsT, r=rhs, st=start, sp=stop:
               e.matmul(a, l, r, start=st, stop=sp), waits=w)

    def t_ms():
        return P.ms("tensor", s_t)

    def v_ms():
        return P.ms("vector", s_v)

    def sc_ms():
        return P.ms("scalar", s_sc)

    V = lambda fn, waits=(), inc=None: P.emit("vector", fn, waits, inc)
    S = lambda fn, waits=(), inc=None: P.emit("scalar", fn, waits, inc)
    _vch = [[]]
    _sch = [[]]

    def VC(fn, waits=()):
        P.emit("vector", fn, waits=list(waits) + _vch[0], inc=(s_v, 1))
        _vch[0] = [(s_v, s_v.count)]
        return (s_v, s_v.count)

    def SCC(fn, waits=()):
        P.emit("scalar", fn, waits=list(waits) + _sch[0], inc=(s_sc, 1))
        _sch[0] = [(s_sc, s_sc.count)]
        return (s_sc, s_sc.count)
    T = lambda fn, waits=(), inc=None: P.emit("tensor", fn, waits, inc)
    SY = lambda fn, waits=(), inc=None: P.emit("sync", fn, waits, inc)

    SY(lambda e: e.dma_start(PM[:, :], pmask_d), inc=(s_cst, 16))
    SY(lambda e: e.dma_start(GL[0:1, 0:L], posr_d), inc=(s_cst, 16))
    SY(lambda e: e.dma_start(ANW[:, :], anw_d), inc=(s_cst, 16))
    SY(lambda e: e.dma_start(MNW[:, :], mnw_d), inc=(s_cst, 16))
    cst_ms = (s_cst, 64)
    gw_a = dslot[7].dma(GW[:, 0:DC * AE], kcm(gwa_d, 0, DC, AE, AE))
    gw_f = dslot[7].dma(GW[:, DC * AE:], kcm(gwf_d, 0, DC, FE, FE))

    P.emit("gpsimd", lambda e: e.iota(IOT, [[1, 128]], base=0, channel_multiplier=-1,
                                      allow_small_or_imprecise_dtypes=True), inc=(s_gp, 1))
    gp_iota = (s_gp, s_gp.count)
    VC(lambda e: e.tensor_single_scalar(IDENT, IOT, 0.0, ALU.is_equal), waits=[gp_iota])
    VC(lambda e: e.tensor_single_scalar(TRIL, IOT, 0.0, ALU.is_gt))
    VC(lambda e: e.tensor_scalar(TRIL, TRIL, NEG, None, ALU.mult))
    VC(lambda e: e.tensor_single_scalar(TRIU, IOT, 0.0, ALU.is_lt))
    VC(lambda e: e.tensor_scalar(TRIU, TRIU, NEG, None, ALU.mult))
    for apx, val in ((C_NEGHALF, -0.5), (C_EPS, EPS), (C_INVD, 1.0 / D),
                     (C_LOGB, -math.log(10000.0) / 64.0), (C_I2PI, 1.0 / (2 * math.pi)),
                     (C_N2PI, -2 * math.pi), (C_HALFPI, math.pi / 2.0), (C_NEG1, -1.0),
                     (C_P2PI, 2 * math.pi), (C_PI, math.pi), (C_NPI, -math.pi),
                     (C_SCL, SCL)):
        VC(lambda e, a=apx, v=val: e.memset(a, v))
    VC(lambda e: e.memset(ONER[:, :], 1.0))
    VC(lambda e: e.memset(STT[:, 32:48], 0.0))
    const_v = VC(lambda e: e.memset(ONEC, 1.0))

    O_Q = 0
    O_KF = 16384
    O_VT = 24576
    O_AO = 32768
    O_RING = 40960
    TB = 16384
    TSCR = 18432
    ACC_XT = 0
    ACC_H = 8192
    ACC_XB = 12288
    A_EXPS = 0
    A_PT = 512
    A_MBB = 1536

    arb = lambda c0, c1: AR[:, c0:c1]
    arf = lambda c0, c1: ARf[:, c0:c1]
    accf = lambda c0, c1: acc[:, c0:c1]
    accb = lambda c0, c1: accB[:, c0:c1]

    qsl = lambda hc, c0, n: arb(O_Q + hc * 1024 + c0, O_Q + hc * 1024 + c0 + n)
    kfsl = lambda g, c0, n: arb(O_KF + g * 2048 + c0, O_KF + g * 2048 + c0 + n)
    vtsl = lambda kb, c0, n: arb(O_VT + kb * 512 + c0, O_VT + kb * 512 + c0 + n)
    aosl = lambda hc, c0, n: arb(O_AO + hc * 1024 + c0, O_AO + hc * 1024 + c0 + n)

    def tile_tables(t):
        P.emit("gpsimd", lambda e: e.iota(GF[0:1, 256:320], [[1, 64]], base=0,
                                          channel_multiplier=0,
                                          allow_small_or_imprecise_dtypes=True),
               inc=(s_gp, 1))
        gi = (s_gp, s_gp.count)
        SCC(lambda e: e.activation(GF[0:1, 256:320], GF[0:1, 256:320], AF.Exp,
                                   bias=0.0, scale=C_LOGB[0:1, 0:1]),
            waits=[gi, const_v, cst_ms])
        SCC(lambda e: e.activation(IFD[0:1, 0:64], GF[0:1, 256:320], AF.Identity,
                                   bias=0.0, scale=1.0))
        if_ms = SCC(lambda e: e.activation(IFD[0:1, 64:128], GF[0:1, 256:320], AF.Identity,
                                           bias=0.0, scale=1.0))
        b = 6
        mm(psb[b][:, :], IFD[0:1, 0:128], GL[0:1, t * 512:(t + 1) * 512],
           True, True, waits=[if_ms, cst_ms], bank=b)
        ang_ms = t_ms()
        SA = arf(TSCR, TSCR + 512)
        SY_ = arf(TSCR + 512, TSCR + 1024)
        prev_sin = []
        for kind, dstc in (("cos", TB + 0), ("sin", TB + 512)):
            if kind == "cos":
                V(lambda e: e.tensor_scalar(SA, psb[b][:, :], C_HALFPI[:, 0:1], None, ALU.add),
                  waits=[ang_ms] + prev_sin)
            else:
                V(lambda e: e.tensor_scalar(SA, psb[b][:, :], 0.0, None, ALU.add),
                  waits=[ang_ms] + prev_sin)
            V(lambda e: e.tensor_scalar(SY_, SA, C_I2PI[:, 0:1], None, ALU.mult))
            V(lambda e: e.tensor_copy(ITAB[:, 0:512], SY_))
            V(lambda e: e.tensor_copy(SY_, ITAB[:, 0:512]))
            V(lambda e: e.scalar_tensor_tensor(SA, SY_, C_N2PI[:, 0:1], SA,
                                               ALU.mult, ALU.add))
            V(lambda e: e.tensor_scalar(SY_, SA, C_PI[:, 0:1], None, ALU.is_gt))
            V(lambda e: e.scalar_tensor_tensor(SA, SY_, C_N2PI[:, 0:1], SA,
                                               ALU.mult, ALU.add))
            V(lambda e: e.tensor_scalar(SY_, SA, C_NPI[:, 0:1], None, ALU.is_lt))
            V(lambda e: e.scalar_tensor_tensor(SA, SY_, C_P2PI[:, 0:1], SA,
                                               ALU.mult, ALU.add), inc=(s_v, 1))
            red = (s_v, s_v.count)
            S(lambda e, dstc=dstc: e.activation(arf(dstc, dstc + 512), SA, AF.Sin,
                                                bias=0.0, scale=1.0),
              waits=[red], inc=(s_sc, 1))
            prev_sin = [(s_sc, s_sc.count)]
        sins = (s_sc, s_sc.count)
        V(lambda e: e.tensor_scalar(ARf[0:64, TB + 1024:TB + 1536],
                                    ARf[0:64, TB + 512:TB + 1024],
                                    C_NEG1[0:64, 0:1], None, ALU.mult), waits=[sins])
        V(lambda e: e.tensor_copy(ARf[64:128, TB + 1024:TB + 1536],
                                  ARf[64:128, TB + 512:TB + 1024]), inc=(s_v, 1))
        tab_ms = (s_v, s_v.count)
        ps_guard[b] = [tab_ms]
        return tab_ms

    def rope_evac(psrc, psw, ctab, stab, dst, wlist, tmp=None):
        if norope == 1:
            V(lambda e: e.tensor_copy(dst, psrc), waits=wlist, inc=(s_v, 1))
            return (s_v, s_v.count)
        if norope == 2:
            V(lambda e: e.tensor_copy(dst, psw), waits=wlist, inc=(s_v, 1))
            return (s_v, s_v.count)
        if norope == 3:
            V(lambda e: e.tensor_copy(dst, tmp), waits=wlist, inc=(s_v, 1))
            return (s_v, s_v.count)
        T1 = arf(TSCR + 1024, TSCR + 1536)
        T2 = arf(TSCR + 1536, TSCR + 2048)
        V(lambda e: e.tensor_tensor(T1, psrc, ctab, ALU.mult), waits=wlist)
        V(lambda e: e.tensor_tensor(T2, psw, stab, ALU.mult))
        V(lambda e: e.tensor_tensor(dst, T1, T2, ALU.add), inc=(s_v, 1))
        return (s_v, s_v.count)

    def ln_tile(t, prev_done):
        own = t < 2
        xt = accf(ACC_XT, ACC_XT + 8192)
        xcol = lambda kc: accf(ACC_XT + kc * 512, ACC_XT + (kc + 1) * 512)
        SY(lambda e, t=t: e.dma_start(
            xt, kcm(xin, t * 512, DC, 512, L)),
            waits=prev_done, inc=(s_x[t % 2], 16))
        x_ms = (s_x[t % 2], 16 * (t // 2 + 1))
        xb = accb(2 * ACC_XB, 2 * ACC_XB + 8192)
        V(lambda e: e.tensor_copy(xb, xt), waits=[x_ms], inc=(s_v, 1))
        xb_ms = (s_v, s_v.count)
        b_sum, b_sq = 6, 7
        for kc in range(DC):
            mm(psb[b_sum][0:1, :], ONEC,
               accb(2 * ACC_XB + kc * 512, 2 * ACC_XB + (kc + 1) * 512),
               kc == 0, kc == DC - 1, waits=[xb_ms] if kc == 0 else (), bank=b_sum)
        sum_ms = t_ms()
        S(lambda e: e.activation(xb, xt, AF.Square, bias=0.0, scale=1.0),
          waits=[sum_ms], inc=(s_sc, 1))
        sq_ms = (s_sc, s_sc.count)
        for kc in range(DC):
            mm(psb[b_sq][0:1, :], ONEC,
               accb(2 * ACC_XB + kc * 512, 2 * ACC_XB + (kc + 1) * 512),
               kc == 0, kc == DC - 1, waits=[sq_ms] if kc == 0 else (), bank=b_sq)
        sqmm_ms = t_ms()
        RM = ARf[0:1, TSCR:TSCR + 512]
        RV = ARf[0:1, TSCR + 512:TSCR + 1024]
        RR = ARf[0:1, TSCR + 1024:TSCR + 1536]
        RMR = ARf[0:1, TSCR + 1536:TSCR + 2048]
        R2 = RM
        C0E = C_EPS[0:1, 0:1]
        C0H = C_NEGHALF[0:1, 0:1]
        VC(lambda e: e.tensor_scalar(RM, psb[b_sum][0:1, :], C_INVD[0:1, 0:1], None, ALU.mult),
           waits=[sum_ms])
        ps_guard[b_sum] = [(s_v, s_v.count)]
        VC(lambda e: e.tensor_scalar(RV, psb[b_sq][0:1, :], C_INVD[0:1, 0:1], None, ALU.mult),
           waits=[sqmm_ms])
        ps_guard[b_sq] = [(s_v, s_v.count)]
        VC(lambda e: e.tensor_tensor(RMR, RM, RM, ALU.mult))
        VC(lambda e: e.tensor_tensor(RV, RV, RMR, ALU.subtract))
        vr = VC(lambda e: e.tensor_scalar(RR, RV, C0E, None, ALU.add))
        SCC(lambda e: e.activation(RR, RR, AF.Ln, bias=0.0, scale=1.0), waits=[vr])
        r_ms = SCC(lambda e: e.activation(RR, RR, AF.Exp, bias=0.0, scale=C0H))
        rows_ms = VC(lambda e: e.tensor_tensor(RMR, RM, RR, ALU.mult), waits=[r_ms])
        VC(lambda e: e.tensor_scalar(R2, RV, C0E, None, ALU.add))
        VC(lambda e: e.reciprocal(R2, R2))
        VC(lambda e: e.tensor_tensor(R2, RV, R2, ALU.mult))
        r2v = VC(lambda e: e.tensor_scalar(R2, R2, C0E, None, ALU.add))
        SCC(lambda e: e.activation(R2, R2, AF.Ln, bias=0.0, scale=1.0), waits=[r2v])
        r2_ms = SCC(lambda e: e.activation(R2, R2, AF.Exp, bias=0.0, scale=C0H))
        mm(psb[4][:, :], ONER[0:1, 0:128], RR, True, True, waits=[rows_ms, r_ms], bank=4)
        mm(psb[5][:, :], ONER[0:1, 0:128], RMR, True, True, waits=[rows_ms], bank=5)
        bc_ms = t_ms()
        mm(psb[6][:, :], ONER[0:1, 0:128], R2, True, True, waits=[r2_ms], bank=6)
        bc2_ms = t_ms()
        for kc in range(DC):
            if own:
                xdst = xnb[:, kc * 1024 + t * 512: kc * 1024 + (t + 1) * 512]
            else:
                xdst = accb(2 * ACC_XB + kc * 512, 2 * ACC_XB + (kc + 1) * 512)
            hdst = accb(2 * ACC_H + kc * 512, 2 * ACC_H + (kc + 1) * 512)
            V(lambda e, kc=kc, xdst=xdst: e.tensor_tensor(xdst, xcol(kc), psb[4][:, :],
                                                          ALU.mult),
              waits=[bc_ms] if kc == 0 else ())
            V(lambda e, xdst=xdst: e.tensor_tensor(xdst, xdst, psb[5][:, :], ALU.subtract))
            V(lambda e, xdst=xdst, hdst=hdst: e.tensor_tensor(hdst, xdst, psb[6][:, :],
                                                              ALU.mult),
              waits=[bc2_ms] if kc == 0 else ())
            V(lambda e, kc=kc, hdst=hdst: e.tensor_scalar(hdst, hdst, ANW[:, kc:kc + 1],
                                                          None, ALU.mult))
        h_ms = v_ms()
        ps_guard[4] = [h_ms]
        ps_guard[5] = [h_ms]
        ps_guard[6] = [h_ms]
        return h_ms

    hcol = lambda kc: accb(2 * ACC_H + kc * 512, 2 * ACC_H + (kc + 1) * 512)
    hblk = lambda kc, tb: accB[:, 2 * ACC_H + kc * 512 + tb * 128:
                               2 * ACC_H + kc * 512 + tb * 128 + 128]

    def rslot(i):
        return arb(O_RING + i * 4096, O_RING + (i + 1) * 4096)

    prev_done = []
    for t in range(4):
        own = t < 2
        tab_ms = tile_tables(t)
        h_ms = ln_tile(t, prev_done)
        CTAB = arf(TB, TB + 512)
        STAB = arf(TB + 1024, TB + 1536)
        for kc in range(DC):
            sl = kc % 2
            slab = rslot(sl)[:, 0:1024]
            slab_ms = dslot[sl].dma(slab, wk_d[kc * 128:(kc + 1) * 128, :])
            for dc in range(NKV):
                mm(psb[dc][:, 0:512], slab[:, dc * 256:dc * 256 + 128], hcol(kc),
                   kc == 0, kc == DC - 1, waits=[h_ms, slab_ms], bank=dc)
                mm(psb[4 + dc][:, 0:512], slab[:, dc * 256 + 128:dc * 256 + 256], hcol(kc),
                   kc == 0, kc == DC - 1, waits=[h_ms, slab_ms], bank=4 + dc)
            dslot[sl].guard = [t_ms()]
        kmm_ms = t_ms()
        for dc in range(NKV):
            r2 = rope_evac(psb[dc][:, :], psb[4 + dc][:, :], CTAB, STAB,
                           kfsl(dc, t * 512, 512), [kmm_ms, tab_ms])
            ps_guard[dc] = [r2]
            ps_guard[4 + dc] = [r2]
        for kc in range(DC):
            sl = kc % 2
            slab = rslot(sl)[:, 0:512]
            slab_ms = dslot[sl].dma(slab, wv_d[kc * 128:(kc + 1) * 128, :])
            for tb in range(4):
                mm(psb[tb][:, 0:512], hblk(kc, tb), slab,
                   kc == 0, kc == DC - 1, waits=[h_ms, slab_ms], bank=tb)
            dslot[sl].guard = [t_ms()]
        vmm_ms = t_ms()
        for tb in range(4):
            kb = t * 4 + tb
            V(lambda e, tb=tb, kb=kb: e.tensor_copy(vtsl(kb, 0, 512), psb[tb][:, :]),
              waits=[vmm_ms], inc=(s_v, 1))
            ps_guard[tb] = [(s_v, s_v.count)]
        if own:
            QC = arf(TB + 1536, TB + 2048)
            V(lambda e: e.tensor_scalar(QC, CTAB, C_SCL[:, 0:1], None, ALU.mult),
              waits=[tab_ms])
            QS = arf(TSCR + 512, TSCR + 1024)
            V(lambda e: e.tensor_scalar(QS, STAB, C_SCL[:, 0:1], None, ALU.mult),
              inc=(s_v, 1))
            qtab_ms = (s_v, s_v.count)
            if t == 0 and "tab" in dbg_t:
                SY(lambda e: e.dma_start(dbg_t["tab"], arf(TB, TB + 2048)),
                   waits=[qtab_ms], inc=(s_out, 16))
                SY(lambda e: e.dma_start(dbg_t["qs"], arf(TSCR + 512, TSCR + 1024)),
                   waits=[qtab_ms], inc=(s_out, 16))
            for hc in range(NH):
                sl = hc % 2
                slab = rslot(sl)
                slab_ms = dslot[sl].dma(slab, kcm(wq_d, hc * 256, DC, 256, 2 * D))
                for kc in range(DC):
                    mm(psb[5][:, 0:512], slab[:, kc * 256:kc * 256 + 128], hcol(kc),
                       kc == 0, kc == DC - 1, waits=[h_ms, slab_ms], bank=5)
                for kc in range(DC):
                    mm(psb[6][:, 0:512], slab[:, kc * 256 + 128:kc * 256 + 256], hcol(kc),
                       kc == 0, kc == DC - 1, waits=[h_ms, slab_ms], bank=6)
                qmm = t_ms()
                dslot[sl].guard = [qmm]
                qr = rope_evac(psb[5][:, :], psb[6][:, :], QC, QS,
                               qsl(hc, t * 512, 512), [qmm, qtab_ms])
                ps_guard[5] = [qr]
                ps_guard[6] = [qr]
        prev_done = [v_ms(), sc_ms()]

    p1_v = v_ms()

    def dump_fm(name, src_fn, nchunk, ms):
        if name not in dbg_t:
            return
        for kc in range(nchunk):
            for c2 in range(2):
                V(lambda e, kc=kc, c2=c2: e.tensor_copy(
                    GF[:, 0:512] if False else GF[:, 0:512],
                    src_fn(kc)[:, c2 * 512:(c2 + 1) * 512]), waits=[ms], inc=(s_v, 1))
                m = (s_v, s_v.count)
                SY(lambda e, kc=kc, c2=c2, name=name: e.dma_start(
                    dbg_t[name][kc * 128:(kc + 1) * 128, c2 * 512:(c2 + 1) * 512],
                    GF[:, 0:512]), waits=[m], inc=(s_out, 16))
                V(lambda e: e.tensor_copy(STT[0:1, 32:33], STT[0:1, 32:33]),
                  waits=[(s_out, s_out.count)])

    dump_fm("xn", lambda kc: xnb[:, kc * 1024:(kc + 1) * 1024], DC, p1_v)
    dump_fm("q", lambda kc: arb(O_Q + kc * 1024, O_Q + (kc + 1) * 1024), NH, p1_v)
    dump_fm("kf", lambda kc: arb(O_KF + kc * 1024, O_KF + (kc + 1) * 1024), NKV * 2, p1_v)
    if "kf" in dbg_t:
        pass
    dump_fm("vt", lambda kc: arb(O_VT + kc * 1024, O_VT + (kc + 1) * 1024), 8, p1_v)

    def pass1(h, hbuf):
        g = h // 4
        prev_mbt = []
        p1g = {0: [], 1: []}
        exp_guard = {0: [], 1: []}
        for j in range(8):
            cb = 16 * (j % 2)
            C = lambda i: STT[:, cb + i: cb + i + 1]
            pieces = []
            nk = (j + 1) * 128
            for seg in range(2):
                off = 0
                while off < nk:
                    wp = min(512, nk - off)
                    bank = 4 + (len(pieces) % 4)
                    mm(psb[bank][:, 0:wp], qsl(h, j * 128, 128),
                       kfsl(g, seg * 1024 + off, wp), True, True,
                       waits=[p1_v], bank=bank)
                    wl = [t_ms()]
                    dlo = j * 128
                    if off <= dlo < off + wp:
                        mt = TRIL if seg == 0 else PM[:, :]
                        V(lambda e, bank=bank, o=dlo - off, mt=mt: e.tensor_tensor(
                            psb[bank][:, o:o + 128], psb[bank][:, o:o + 128], mt, ALU.add),
                          waits=wl + [const_v, cst_ms], inc=(s_v, 1))
                        wl = [(s_v, s_v.count)]
                    pi = len(pieces)
                    VC(lambda e, bank=bank, pi=pi, wp=wp, cb=cb: e.tensor_reduce(
                        STT[:, cb + pi:cb + pi + 1], psb[bank][:, 0:wp], AX.X, ALU.max),
                       waits=wl + p1g[j % 2])
                    pieces.append((bank, wp, (s_v, s_v.count)))
                    off += wp
            npc = len(pieces)
            VC(lambda e, cb=cb: e.tensor_copy(STT[:, cb+8:cb+9], STT[:, cb:cb+1]),
               waits=[pieces[0][2]])
            for pi in range(1, npc):
                VC(lambda e, pi=pi, cb=cb: e.tensor_tensor(STT[:, cb+8:cb+9],
                   STT[:, cb+8:cb+9], STT[:, cb+pi:cb+pi+1], ALU.max),
                   waits=[pieces[pi][2]])
            negM = VC(lambda e, cb=cb: e.tensor_scalar(STT[:, cb+9:cb+10],
                      STT[:, cb+8:cb+9], C_NEG1[:, 0:1], None, ALU.mult))
            for pi, (bank, wp, _) in enumerate(pieces):
                eb = pi % 2
                S(lambda e, bank=bank, wp=wp, pi=pi, cb=cb, eb=eb: e.activation(
                    accb(2 * A_EXPS + eb * 512, 2 * A_EXPS + eb * 512 + wp),
                    psb[bank][:, 0:wp], AF.Exp,
                    bias=STT[:, cb+9:cb+10], scale=1.0),
                  waits=[negM] + exp_guard[eb], inc=(s_sc, 1))
                ps_guard[bank] = [(s_sc, s_sc.count)]
                VC(lambda e, wp=wp, pi=pi, cb=cb, eb=eb: e.tensor_reduce(
                    STT[:, cb + 4 + pi:cb + 5 + pi],
                    accb(2 * A_EXPS + eb * 512, 2 * A_EXPS + eb * 512 + wp),
                    AX.X, ALU.add),
                   waits=[(s_sc, s_sc.count)])
                exp_guard[eb] = [(s_v, s_v.count)]
            VC(lambda e, cb=cb: e.tensor_copy(STT[:, cb+10:cb+11], STT[:, cb+4:cb+5]))
            for pi in range(1, npc):
                VC(lambda e, pi=pi, cb=cb: e.tensor_tensor(STT[:, cb+10:cb+11],
                   STT[:, cb+10:cb+11], STT[:, cb+4+pi:cb+5+pi], ALU.add))
            st_ms = VC(lambda e, cb=cb: e.tensor_copy(STT[:, cb+12:cb+13],
                       STT[:, cb+10:cb+11]), waits=list(prev_mbt))
            SCC(lambda e, cb=cb: e.activation(STT[:, cb+12:cb+13], STT[:, cb+12:cb+13],
                                              AF.Ln, bias=0.0, scale=1.0), waits=[st_ms])
            mb_ms = SCC(lambda e, cb=cb: e.activation(STT[:, cb+12:cb+13],
                        STT[:, cb+12:cb+13], AF.Identity, bias=STT[:, cb+8:cb+9],
                        scale=1.0))
            p1g[j % 2] = [mb_ms]
            if h == 4 and j == 1 and "p1t" in dbg_t:
                SY(lambda e: e.dma_start(dbg_t["p1t"], STT[:, 0:16]),
                   waits=[mb_ms], inc=(s_out, 16))
            tb = 4 + (j % 4)
            T(lambda e, tb=tb, cb=cb: e.transpose(psb[tb][0:1, 0:128], STT[:, cb+12:cb+13], IDENT),
              waits=[mb_ms] + ps_guard[tb])
            tms = t_ms()
            prev_mbt = [tms]
            V(lambda e, tb=tb, hbuf=hbuf, j=j: e.tensor_copy(
                GL[0:1, hbuf * 1024 + j * 128: hbuf * 1024 + (j + 1) * 128],
                psb[tb][0:1, 0:128]), waits=[tms], inc=(s_v, 1))
            ps_guard[tb] = [(s_v, s_v.count)]
        if "mbr" in dbg_t:
            mm_ = v_ms()
            SY(lambda e, h=h, hbuf=hbuf: e.dma_start(
                dbg_t["mbr"][h:h + 1, :], GL[0:1, hbuf * 1024:(hbuf + 1) * 1024]),
               waits=[mm_], inc=(s_out, 16))
        return v_ms()

    pt_guard = [[], []]

    def pass2(h, hbuf, mb_ready):
        g = h // 4
        mbb0 = A_MBB + hbuf * 1024
        for c2 in range(2):
            mm(psb[6][:, :], ONER[0:1, 0:128],
               GL[0:1, hbuf * 1024 + c2 * 512: hbuf * 1024 + (c2 + 1) * 512],
               True, True, waits=[mb_ready], bank=6)
            bmms = t_ms()
            V(lambda e, c2=c2: e.tensor_copy(accf(mbb0 + c2 * 512, mbb0 + (c2 + 1) * 512),
                                             psb[6][:, :]), waits=[bmms], inc=(s_v, 1))
            ps_guard[6] = [(s_v, s_v.count)]
        mbb_ms = (s_v, s_v.count)
        contrib = {0: [], 1: []}
        for kc in range(16):
            qlo = (kc % 8) * 128
            for qt in range(2):
                if qlo < (qt + 1) * 512:
                    contrib[qt].append(kc)
        for kc in range(16):
            seg, m8 = kc // 8, kc % 8
            qlo = m8 * 128
            ptb = kc % 2
            ptbuf = accB[:, 2 * A_PT + ptb * 1024: 2 * A_PT + ptb * 1024 + 1024]
            exp_parts = []
            off = qlo
            while off < 1024:
                wt = min(512, 1024 - off)
                bank = kc % 2
                mm(psb[bank][:, 0:wt], kfsl(g, seg * 1024 + m8 * 128, 128),
                   qsl(h, off, wt), True, True, waits=[p1_v], bank=bank)
                wl = [t_ms(), mbb_ms]
                if off == qlo:
                    mt = TRIU if seg == 0 else PM[:, :]
                    V(lambda e, bank=bank, mt=mt: e.tensor_tensor(
                        psb[bank][:, 0:128], psb[bank][:, 0:128], mt, ALU.add),
                      waits=wl, inc=(s_v, 1))
                    wl = [(s_v, s_v.count)]
                V(lambda e, bank=bank, wt=wt, off=off: e.tensor_tensor(
                    psb[bank][:, 0:wt], psb[bank][:, 0:wt],
                    accf(mbb0 + off, mbb0 + off + wt), ALU.subtract),
                  waits=wl, inc=(s_v, 1))
                sub_ms = (s_v, s_v.count)
                S(lambda e, bank=bank, wt=wt, off=off, qlo=qlo, ptbuf=ptbuf: e.activation(
                    ptbuf[:, off - qlo:off - qlo + wt], psb[bank][:, 0:wt], AF.Exp,
                    bias=0.0, scale=1.0),
                  waits=[sub_ms] + pt_guard[ptb], inc=(s_sc, 1))
                ps_guard[bank] = [(s_sc, s_sc.count)]
                exp_parts.append((s_sc, s_sc.count))
                off += wt
            for qt in range(2):
                q0 = qt * 512
                if kc not in contrib[qt]:
                    continue
                lo = max(q0, qlo)
                wt = q0 + 512 - lo
                bank = 2 + qt
                first = kc == contrib[qt][0]
                last = kc == contrib[qt][-1]
                wl = list(exp_parts)
                if first:
                    wl += ps_guard[bank]
                mm(psb[bank][:, lo - q0:lo - q0 + wt], vtsl(kc, g * 128, 128),
                   ptbuf[:, lo - qlo:lo - qlo + wt], first, last, waits=wl)
            pt_guard[ptb] = [t_ms()]
        pv_ms = t_ms()
        for qt in range(2):
            V(lambda e, qt=qt: e.tensor_copy(aosl(h, qt * 512, 512), psb[2 + qt][:, :]),
              waits=[pv_ms], inc=(s_v, 1))
            ps_guard[2 + qt] = [(s_v, s_v.count)]
        return v_ms()

    mb_prev = None
    for h in range(NH):
        hbuf = h % 2
        mb_now = pass1(h, hbuf)
        pass2(h, hbuf, mb_now)
    ao_ms = v_ms()

    dump_fm("ao", lambda kc: arb(O_AO + kc * 1024, O_AO + (kc + 1) * 1024), DC, ao_ms)

    attn_end_t = t_ms()
    for i in range(4):
        dslot[i].guard = [attn_end_t]
    def rslot2(i):
        return arb(i * 2048, (i + 1) * 2048)
    for hc in range(DC):
        sl = hc % 4
        slab = rslot2(sl)
        slab_ms = dslot[sl].dma(slab, kcm(wo_d, hc * 128, DC, 128, D))
        for tt in range(2):
            bank = tt
            for kc in range(DC):
                mm(psb[bank][:, :], slab[:, kc * 128:(kc + 1) * 128],
                   aosl(kc, tt * 512, 512), kc == 0, kc == DC - 1,
                   waits=[ao_ms, slab_ms], bank=bank)
            womm = t_ms()
            V(lambda e, hc=hc, tt=tt, bank=bank: e.tensor_tensor(
                accf(hc * 1024 + tt * 512, hc * 1024 + tt * 512 + 512), psb[bank][:, :],
                xnb[:, hc * 1024 + tt * 512: hc * 1024 + tt * 512 + 512], ALU.add),
              waits=[womm], inc=(s_v, 1))
            ps_guard[bank] = [(s_v, s_v.count)]
        dslot[sl].guard = [t_ms()]
    acc1_ms = v_ms()

    def dump_acc(name, ms):
        if name not in dbg_t:
            return
        for kc in range(DC):
            SY(lambda e, kc=kc, name=name: e.dma_start(
                dbg_t[name][kc * 128:(kc + 1) * 128, :], accf(kc * 1024, (kc + 1) * 1024)),
                waits=[ms], inc=(s_out, 16))

    dump_acc("acc1", acc1_ms)

    O_HM = 0
    O_ACTA = 16384
    O_RA = 24576
    O_W2A = 32768
    for tt in range(2):
        b_s = 6
        for kc in range(DC):
            sqw = [acc1_ms] if kc == 0 else [t_ms()]
            S(lambda e, kc=kc, tt=tt: e.activation(
                GFB[:, 0:512], accf(kc * 1024 + tt * 512, kc * 1024 + tt * 512 + 512),
                AF.Square, bias=0.0, scale=1.0), waits=sqw, inc=(s_sc, 1))
            sqm = (s_sc, s_sc.count)
            mm(psb[b_s][0:1, :], ONEC, GFB[:, 0:512], kc == 0, kc == DC - 1,
               waits=[sqm], bank=b_s)
        smm = t_ms()
        RO = GL[0:1, 0:512]
        V(lambda e, b_s=b_s: e.tensor_scalar(RO, psb[b_s][0:1, :], C_INVD[0:1, 0:1],
                                             None, ALU.mult), waits=[smm], inc=(s_v, 1))
        ps_guard[b_s] = [(s_v, s_v.count)]
        V(lambda e: e.tensor_scalar(RO, RO, C_EPS[0:1, 0:1], None, ALU.add), inc=(s_v, 1))
        rv = (s_v, s_v.count)
        S(lambda e: e.activation(RO, RO, AF.Ln, bias=0.0, scale=1.0), waits=[rv])
        S(lambda e: e.activation(RO, RO, AF.Exp, bias=0.0, scale=C_NEGHALF[0:1, 0:1]),
          inc=(s_sc, 1))
        rr = (s_sc, s_sc.count)
        mm(psb[7][:, :], ONER[0:1, 0:128], RO, True, True, waits=[rr], bank=7)
        bc = t_ms()
        for kc in range(DC):
            dst = arb(O_HM + kc * 1024 + tt * 512, O_HM + kc * 1024 + tt * 512 + 512)
            V(lambda e, kc=kc, tt=tt, dst=dst: e.tensor_tensor(
                dst, accf(kc * 1024 + tt * 512, kc * 1024 + tt * 512 + 512),
                psb[7][:, :], ALU.mult), waits=[bc] if kc == 0 else ())
            V(lambda e, kc=kc, dst=dst: e.tensor_scalar(dst, dst, MNW[:, kc:kc + 1],
                                                        None, ALU.mult))
        hm_t = v_ms()
        ps_guard[7] = [hm_t]
    hm_ms = v_ms()
    dump_fm("hm", lambda kc: arb(O_HM + kc * 1024, O_HM + (kc + 1) * 1024), DC, hm_ms)

    for src, gofs, ne, dst0 in ((0, 0, AE, 0), (1, DC * AE, FE, 1024)):
        for tt in range(2):
            bank = 6
            for kc in range(DC):
                rhs = (arb(O_HM + kc * 1024 + tt * 512, O_HM + kc * 1024 + tt * 512 + 512)
                       if src == 0 else
                       xnb[:, kc * 1024 + tt * 512: kc * 1024 + tt * 512 + 512])
                mm(psb[bank][0:ne, :], GW[:, gofs + kc * ne: gofs + (kc + 1) * ne], rhs,
                   kc == 0, kc == DC - 1,
                   waits=[hm_ms, gw_a, gw_f], bank=bank)
            gmm = t_ms()
            V(lambda e, ne=ne, dst0=dst0, tt=tt, bank=bank: e.tensor_copy(
                GL[0:ne, dst0 + tt * 512: dst0 + tt * 512 + 512], psb[bank][0:ne, :]),
              waits=[gmm], inc=(s_v, 1))
            ps_guard[bank] = [(s_v, s_v.count)]
    glog_ms = v_ms()

    for tile in range(8):
        bank = 6
        T(lambda e, tile=tile, bank=bank: e.transpose(
            psb[bank][0:128, 0:8], GL[0:8, tile * 128:(tile + 1) * 128], IDENT[0:8, 0:8]),
          waits=[glog_ms] + ps_guard[bank])
        tm1 = t_ms()
        lg = GTM[:, tile * 8:(tile + 1) * 8]
        VC(lambda e, lg=lg, bank=bank: e.tensor_copy(lg, psb[bank][0:128, 0:8]),
           waits=[tm1])
        ps_guard[bank] = [(s_v, s_v.count)]
        mneg = VC(lambda e, lg=lg: e.tensor_reduce(STT[:, 16:17], lg, AX.X, ALU.max,
                                                   negate=True))
        e_ms = SCC(lambda e, lg=lg: e.activation(lg, lg, AF.Exp, bias=STT[:, 16:17],
                                                 scale=1.0), waits=[mneg])
        cmb = GTM[:, 64 + tile * 8: 64 + (tile + 1) * 8]
        VC(lambda e, lg=lg: e.tensor_reduce(STT[:, 17:18], lg, AX.X, ALU.max),
           waits=[e_ms])
        VC(lambda e, lg=lg, cmb=cmb: e.tensor_scalar(cmb, lg, STT[:, 17:18], None,
                                                     ALU.is_equal))
        VC(lambda e, cmb=cmb: e.tensor_scalar(GF[:, 256:264], cmb, -1.0, 1.0,
                                              ALU.mult, ALU.add))
        VC(lambda e, lg=lg: e.tensor_tensor(GF[:, 256:264], lg, GF[:, 256:264], ALU.mult))
        VC(lambda e: e.tensor_reduce(STT[:, 18:19], GF[:, 256:264], AX.X, ALU.max))
        VC(lambda e: e.tensor_scalar(GF[:, 264:272], GF[:, 256:264], STT[:, 18:19],
                                     None, ALU.is_equal))
        VC(lambda e, cmb=cmb: e.tensor_tensor(cmb, cmb, GF[:, 264:272], ALU.add))
        VC(lambda e: e.tensor_tensor(STT[:, 17:18], STT[:, 17:18], STT[:, 18:19], ALU.add))
        VC(lambda e: e.reciprocal(STT[:, 17:18], STT[:, 17:18]))
        VC(lambda e, lg=lg, cmb=cmb: e.tensor_tensor(cmb, cmb, lg, ALU.mult))
        VC(lambda e, cmb=cmb: e.tensor_scalar(cmb, cmb, STT[:, 17:18], None, ALU.mult))
        T(lambda e, tile=tile, bank=bank: e.transpose(
            psb[bank][0:128, 0:16], GL[0:16, 1024 + tile * 128:1024 + (tile + 1) * 128],
            IDENT[0:16, 0:16]), waits=[glog_ms] + ps_guard[bank])
        tm2 = t_ms()
        VC(lambda e, bank=bank: e.tensor_copy(GF[:, 0:16], psb[bank][0:128, 0:16]),
           waits=[tm2])
        ps_guard[bank] = [(s_v, s_v.count)]
        fneg = VC(lambda e: e.tensor_reduce(STT[:, 20:21], GF[:, 0:16], AX.X, ALU.max,
                                            negate=True))
        fe_ms = SCC(lambda e: e.activation(GF[:, 0:16], GF[:, 0:16], AF.Exp,
                                           bias=STT[:, 20:21], scale=1.0), waits=[fneg])
        VC(lambda e: e.tensor_reduce(STT[:, 21:22], GF[:, 0:16], AX.X, ALU.add),
           waits=[fe_ms])
        VC(lambda e: e.reciprocal(STT[:, 22:23], STT[:, 21:22]))
        VC(lambda e: e.tensor_scalar(GF[:, 0:16], GF[:, 0:16], STT[:, 22:23], None, ALU.mult))
        VC(lambda e: e.tensor_copy(GF[:, 256:272], GF[:, 0:16]))
        VC(lambda e: e.memset(GF[:, 272:288], 0.0))
        for it in range(6):
            VC(lambda e: e.tensor_reduce(STT[:, 23:24], GF[:, 256:272], AX.X, ALU.max))
            VC(lambda e: e.tensor_scalar(GF[:, 288:304], GF[:, 256:272], STT[:, 23:24],
                                         None, ALU.is_equal))
            VC(lambda e: e.tensor_tensor(GF[:, 272:288], GF[:, 272:288], GF[:, 288:304],
                                         ALU.add))
            if it < 5:
                VC(lambda e: e.tensor_scalar(GF[:, 288:304], GF[:, 288:304], -1.0, 1.0,
                                             ALU.mult, ALU.add))
                VC(lambda e: e.tensor_tensor(GF[:, 256:272], GF[:, 256:272], GF[:, 288:304],
                                             ALU.mult))
        VC(lambda e, tile=tile: e.tensor_tensor(
            GTM[:, 128 + tile * 16:128 + (tile + 1) * 16], GF[:, 0:16], GF[:, 272:288],
            ALU.mult))
    gates_ms = v_ms()

    def cbcast(col_fn, cslot, waits):
        for tile in range(8):
            T(lambda e, tile=tile: e.transpose(psb[7][0:1, 0:128], col_fn(tile), IDENT),
              waits=list(waits) + ps_guard[7])
            tms = t_ms()
            V(lambda e, tile=tile: e.tensor_copy(
                GL[0:1, tile * 128:(tile + 1) * 128], psb[7][0:1, 0:128]),
              waits=[tms], inc=(s_v, 1))
            ps_guard[7] = [(s_v, s_v.count)]
        row_ms = v_ms()
        for c2 in range(2):
            mm(psb[7][:, :], ONER[0:1, 0:128], GL[0:1, c2 * 512:(c2 + 1) * 512],
               True, True, waits=[row_ms], bank=7)
            bm = t_ms()
            V(lambda e, c2=c2: e.tensor_copy(
                CBC[:, c2 * 512: c2 * 512 + 512],
                psb[7][:, :]), waits=[bm], inc=(s_v, 1))
            ps_guard[7] = [(s_v, s_v.count)]
        return v_ms()

    def gu_pairs(npairs, slab_fn, rhs_fn, act_base, cslot, extra):
        for pair in range(npairs):
            slab, slab_ms, dsl = slab_fn(pair)
            for tt in range(2):
                bg = (pair * 2 + tt) % 2
                bu = 2 + (pair * 2 + tt) % 2
                for kc in range(DC):
                    mm(psb[bg][:, :], slab[:, kc * 256: kc * 256 + 128], rhs_fn(kc, tt),
                       kc == 0, kc == DC - 1,
                       waits=([slab_ms] + list(extra)) if kc == 0 else (), bank=bg)
                for kc in range(DC):
                    mm(psb[bu][:, :], slab[:, kc * 256 + 128: kc * 256 + 256],
                       rhs_fn(kc, tt), kc == 0, kc == DC - 1,
                       waits=([slab_ms] + list(extra)) if kc == 0 else (), bank=bu)
                gu_mm = t_ms()
                sg = GFB[:, tt * 512:(tt + 1) * 512]
                S(lambda e, bg=bg, sg=sg: e.activation(sg, psb[bg][:, :], AF.Silu,
                                                       bias=0.0, scale=1.0),
                  waits=[gu_mm], inc=(s_sc, 1))
                sg_ms = (s_sc, s_sc.count)
                ps_guard[bg] = [sg_ms]
                dst = arb(act_base + pair * 1024 + tt * 512,
                          act_base + pair * 1024 + tt * 512 + 512)
                V(lambda e, bu=bu, sg=sg, dst=dst: e.tensor_tensor(
                    dst, psb[bu][:, :], sg, ALU.mult), waits=[sg_ms], inc=(s_v, 1))
                ps_guard[bu] = [(s_v, s_v.count)]
                if cslot is not None:
                    V(lambda e, dst=dst, tt=tt: e.tensor_tensor(
                        dst, dst, CBC[:, tt * 512: tt * 512 + 512], ALU.mult))
            dsl.guard = [t_ms()]
        return t_ms(), v_ms()

    def w2_stage(nic, w2base, act_base, extra):
        for hc in range(DC):
            for tt in range(2):
                bank = 4 + (hc * 2 + tt) % 2
                for ic in range(nic):
                    mm(psb[bank][:, :],
                       arb(w2base + ic * 2048 + hc * 128, w2base + ic * 2048 + hc * 128 + 128),
                       arb(act_base + ic * 1024 + tt * 512,
                           act_base + ic * 1024 + tt * 512 + 512),
                       ic == 0, ic == nic - 1, waits=list(extra) if ic == 0 else (),
                       bank=bank)
                wmm = t_ms()
                V(lambda e, hc=hc, tt=tt, bank=bank: e.tensor_tensor(
                    accf(hc * 1024 + tt * 512, hc * 1024 + tt * 512 + 512),
                    psb[bank][:, :],
                    accf(hc * 1024 + tt * 512, hc * 1024 + tt * 512 + 512), ALU.add),
                  waits=[wmm], inc=(s_v, 1))
                ps_guard[bank] = [(s_v, s_v.count)]
        return v_ms()

    wo_end_t = t_ms()
    for i in (4, 5, 6):
        dslot[i].guard = [wo_end_t]
    hm_rhs = lambda kc, tt: arb(O_HM + kc * 1024 + tt * 512, O_HM + kc * 1024 + tt * 512 + 512)
    for e_ in range(AE):
        cslot = e_ % 2
        cb_ms = cbcast(lambda tile, e_=e_: GTM[:, 64 + tile * 8 + e_: 64 + tile * 8 + e_ + 1],
                       cslot, [gates_ms])

        def slab_a(pair, e_=e_):
            sl = 4 + pair % 2
            dst = arb(O_RA + (pair % 2) * 4096, O_RA + (pair % 2) * 4096 + 4096)
            msd = dslot[sl].dma(dst, kcm(w13a_d, e_ * D * 2 * AI + pair * 256,
                                         DC, 256, 2 * AI))
            return dst, msd, dslot[sl]

        w2a_sl = dslot[6]
        w2a_ms = w2a_sl.dma(arb(O_W2A, O_W2A + 16384),
                            kcm(w2a_d, e_ * AI * D, AIC, D, D))
        lt, lv = gu_pairs(AIC, slab_a, hm_rhs, O_ACTA, cslot, [cb_ms, hm_ms])
        vv = w2_stage(AIC, O_W2A, O_ACTA, [w2a_ms, lv])
        w2a_sl.guard = [t_ms()]
    accA_ms = v_ms()
    dump_acc("accA", accA_ms)

    moea_end_t = t_ms()
    dslot[4].guard = [moea_end_t]
    dslot[5].guard = [moea_end_t]
    O_ACTS = 0
    O_SW1 = 22528
    O_SW2 = 26624
    xn_rhs = lambda kc, tt: xnb[:, kc * 1024 + tt * 512: kc * 1024 + tt * 512 + 512]

    def slab_s(pair):
        dst = arb(O_SW1, O_SW1 + 4096)
        msd = dslot[4].dma(dst, kcm(sw1_d, pair * 256, DC, 256, 2 * SI))
        return dst, msd, dslot[4]

    lt, acts_v = gu_pairs(SIC, slab_s, xn_rhs, O_ACTS, None, [accA_ms])
    sw2_sl = dslot[5]
    for half in range(2):
        sw2_ms = sw2_sl.dma(arb(O_SW2, O_SW2 + 22528),
                            kcm(sw2_d, half * FI * D, FIC, D, D))
        vv = w2_stage(FIC, O_SW2, O_ACTS + half * FIC * 1024, [sw2_ms, acts_v])
        sw2_sl.guard = [t_ms()]
    sh_ms = v_ms()
    dump_acc("sh", sh_ms)

    sh_end_t = t_ms()
    for i in (1, 2, 3):
        dslot[i].guard = [sh_end_t]
    O_ACTF = 0
    O_RF = 11264
    O_W2F = 19456
    for e_ in range(FE):
        cslot = e_ % 2
        cb_ms = cbcast(
            lambda tile, e_=e_: GTM[:, 128 + tile * 16 + e_: 128 + tile * 16 + e_ + 1],
            cslot, [gates_ms])

        def slab_f(pair, e_=e_):
            sl = 2 + pair % 2
            dst = arb(O_RF + (pair % 2) * 4096, O_RF + (pair % 2) * 4096 + 4096)
            msd = dslot[sl].dma(dst, kcm(w1f_d, e_ * D * 2 * FI + pair * 256,
                                         DC, 256, 2 * FI))
            return dst, msd, dslot[sl]

        w2f_sl = dslot[1]
        w2f_ms = w2f_sl.dma(arb(O_W2F, O_W2F + 22528),
                            kcm(w2f_d, e_ * FI * D, FIC, D, D))
        extra = [cb_ms, sh_ms] if e_ == 0 else [cb_ms]
        lt, lv = gu_pairs(FIC, slab_f, xn_rhs, O_ACTF, cslot, extra)
        dslot[2].guard = [lt]
        dslot[3].guard = [lt]
        vv = w2_stage(FIC, O_W2F, O_ACTF, [w2f_ms, lv])
        w2f_sl.guard = [t_ms()]
    accF_ms = v_ms()
    dump_acc("accF", accF_ms)

    x_fill = [2, 2]
    for kc in range(DC):
        buf = kc % 2
        xf = arf(buf * 1024, buf * 1024 + 1024)
        w = [v_ms()] if kc >= 2 else [accF_ms]
        SY(lambda e, kc=kc, xf=xf: e.dma_start(xf, xin[kc * 128:(kc + 1) * 128, 0:1024]),
           waits=w + [accF_ms], inc=(s_x[buf], 16))
        x_fill[buf] += 1
        xf_ms = (s_x[buf], 16 * x_fill[buf])
        ost = arf(2048 + buf * 1024, 2048 + buf * 1024 + 1024)
        V(lambda e, kc=kc, xf=xf, ost=ost: e.tensor_tensor(
            ost, accf(kc * 1024, (kc + 1) * 1024), xf, ALU.add),
          waits=[xf_ms], inc=(s_v, 1))
        om = (s_v, s_v.count)
        SY(lambda e, kc=kc, ost=ost: e.dma_start(out[kc * 128:(kc + 1) * 128, :], ost),
           waits=[om], inc=(s_out, 16))
    fin = (s_out, s_out.count)
    V(lambda e: e.tensor_copy(STT[0:1, 40:41], STT[0:1, 40:41]), waits=[fin])

    with nc.Block() as block:
        P.replay(block)
    es.close()
    return nc



def _relayout(inputs):
    colsel = np.concatenate([np.arange(0, 128, 2), np.arange(1, 128, 2)])
    rotsel = np.concatenate([colsel[64:], colsel[:64]])
    wq_p = np.asarray(inputs["L0_attn_w_q"]).reshape(D, NH, HD)
    wq = np.ascontiguousarray(
        np.stack([wq_p[:, :, colsel], wq_p[:, :, rotsel]], axis=2).reshape(D, 2 * D))
    wk_p = np.asarray(inputs["L0_attn_w_k"]).reshape(D, NKV, HD)
    wk = np.ascontiguousarray(
        np.stack([wk_p[:, :, colsel], wk_p[:, :, rotsel]], axis=2).reshape(D, 2 * NKV * HD))
    wv = np.ascontiguousarray(np.asarray(inputs["L0_attn_w_v"]))
    wo = np.ascontiguousarray(np.asarray(inputs["L0_attn_w_o"]))
    anw = np.ascontiguousarray(np.asarray(inputs["L0_attn_attn_norm_w"]).reshape(DC, 128).T)
    mnw = np.ascontiguousarray(np.asarray(inputs["L0_attn_moe_norm_w"]).reshape(DC, 128).T)
    gwa = np.ascontiguousarray(np.asarray(inputs["L0_attn_gate_w"]))
    w1a = np.asarray(inputs["L0_attn_w1"]).reshape(AE, D, AIC, 128)
    w3a = np.asarray(inputs["L0_attn_w3"]).reshape(AE, D, AIC, 128)
    w13a = np.ascontiguousarray(np.stack([w1a, w3a], axis=3).reshape(AE, D, 2 * AI))
    w2a = np.ascontiguousarray(np.asarray(inputs["L0_attn_w2"]))
    gwf = np.ascontiguousarray(np.asarray(inputs["L0_ffn_gate_w"]).T)
    w1f_t = np.asarray(inputs["L0_ffn_w1"]).transpose(0, 2, 1).reshape(FE, D, 2, FIC, 128)
    w1f = np.ascontiguousarray(w1f_t.transpose(0, 1, 3, 2, 4)).reshape(FE, D, 2 * FI)
    w2f = np.ascontiguousarray(np.asarray(inputs["L0_ffn_w2"]).transpose(0, 2, 1))
    sw1_t = np.asarray(inputs["L0_ffn_shared_w1"]).T.reshape(D, 2, SIC, 128)
    sw1 = np.ascontiguousarray(sw1_t.transpose(0, 2, 1, 3)).reshape(D, 2 * SI)
    sw2 = np.ascontiguousarray(np.asarray(inputs["L0_ffn_shared_w2"]).T)
    return dict(wq=wq, wk=wk, wv=wv, wo=wo, anw=anw, mnw=mnw, gwa=gwa, w13a=w13a,
                w2a=w2a, gwf=gwf, w1f=w1f, w2f=w2f, sw1=sw1, sw2=sw2)


def _percore(x, core):
    seq, par = core // 2, core % 2
    blocks = np.concatenate([np.arange(par, 16, 2), np.arange(1 - par, 16, 2)])
    colpos = (blocks[:, None] * 128 + np.arange(128)[None, :]).reshape(-1)
    xin = np.ascontiguousarray(np.asarray(x[seq])[colpos, :].T)
    posr = np.ascontiguousarray(colpos.astype(np.float32)[None, :])
    pm = (np.zeros((128, 128), np.float32) if par == 1
          else np.full((128, 128), np.float32(NEG)))
    return xin, posr, pm, colpos


_NC_CACHE = {}


def kernel(**inputs):
    if "main" not in _NC_CACHE:
        _NC_CACHE["main"] = build_program()
    nc = _NC_CACHE["main"]
    shared = _relayout(inputs)
    x = np.asarray(inputs["L0_attn_x"])
    in_maps, cols = [], []
    for c in range(8):
        xi, posr, pm, colpos = _percore(x, c)
        m = dict(shared)
        m.update(xin=xi, posr=posr, pmask=pm)
        in_maps.append(m)
        cols.append(colpos)
    res = run_bass_kernel_spmd(nc, in_maps, core_ids=list(range(8)))
    outp = np.empty((4, L, D), np.float32)
    for c in range(8):
        o = res.results[c]["out"]
        outp[c // 2, cols[c][:1024], :] = np.asarray(o).T
    return outp


# version 04 (final)
# speedup vs baseline: 1.0565x; 1.0565x over previous
import contextlib
import math
import numpy as np
import concourse.bass as bass
import concourse.mybir as mybir
from concourse.bass_utils import run_bass_kernel_spmd

F32 = mybir.dt.float32
BF16 = mybir.dt.bfloat16
I32 = mybir.dt.int32
AF = mybir.ActivationFunctionType
ALU = mybir.AluOpType
AX = mybir.AxisListType

D, DC, L, TOK = 2048, 16, 2048, 1024
NH, NKV, HD = 16, 4, 128
EPS = 1e-5
NEG = -1.0e30
SCL = 1.0 / math.sqrt(HD)
AE, AI, AIC = 8, 1024, 8
FE, FI, FIC = 16, 1408, 11
SI, SIC = 2816, 22

ENGS = ("sync", "vector", "scalar", "tensor", "gpsimd")


class Sem:
    __slots__ = ("h", "name", "producer", "dma", "count")

    def __init__(self, h, name, producer, dma=False):
        self.h, self.name, self.producer, self.dma, self.count = h, name, producer, dma, 0


class Prog:
    def __init__(self):
        self.ops = {e: [] for e in ENGS}

    def emit(self, eng, fn, waits=(), inc=None):
        rec = [list(waits), fn, inc]
        if inc is not None:
            inc[0].count += inc[1]
        self.ops[eng].append(rec)
        return rec

    def ms(self, eng, sem):
        ops = self.ops[eng]
        assert ops, f"no ops on {eng}"
        rec = ops[-1]
        if rec[2] is None:
            sem.count += 1
            rec[2] = (sem, 1)
        s = rec[2][0]
        return (s, s.count)

    def replay(self, block):
        def mk(name):
            ops = self.ops[name]

            def body(eng):
                lw = {}
                for waits, fn, inc in ops:
                    for (sem, val) in waits:
                        if lw.get(sem.name, -1) >= val:
                            continue
                        eng.wait_ge(sem.h, val)
                        lw[sem.name] = val
                    inst = fn(eng)
                    if inc is not None:
                        inst.then_inc(inc[0].h, inc[1])
            return body

        block.sync(mk("sync"))
        block.vector(mk("vector"))
        block.scalar(mk("scalar"))
        block.tensor(mk("tensor"))
        block.gpsimd(mk("gpsimd"))


class DSlot:

    def __init__(self, P, sem):
        self.P, self.sem, self.fills, self.guard = P, sem, 0, []

    def dma(self, dst, src):
        w = list(self.guard)
        self.P.emit("gpsimd", lambda e, d=dst, s=src: e.dma_start(d, s), waits=w,
                    inc=(self.sem, 16))
        self.fills += 1
        self.guard = []
        return (self.sem, 16 * self.fills)


def build_program(dbg=(), norope=0):
    nc = bass.Bass("TRN2", target_bir_lowering=False, detect_race_conditions=False)
    P = Prog()
    es = contextlib.ExitStack()


    def kcm(dap, off, k, ncols, rowlen):
        return bass.AP(dap.tensor, dap.offset + off,
                       [[rowlen, 128], [128 * rowlen, k], [1, ncols]])

    def din(name, shape, dtype=F32):
        return nc.dram_tensor(name, shape, dtype, kind="ExternalInput").ap()

    xin = din("xin", [D, L])
    wq_d = din("wq", [D, 2 * D])
    wk_d = din("wk", [D, 2 * NKV * HD])
    wv_d = din("wv", [D, NKV * HD])
    wo_d = din("wo", [D, D])
    anw_d = din("anw", [128, DC])
    mnw_d = din("mnw", [128, DC])
    gwa_d = din("gwa", [D, AE])
    w13a_d = din("w13a", [AE, D, 2 * AI])
    w2a_d = din("w2a", [AE, AI, D])
    gwf_d = din("gwf", [D, FE])
    w1f_d = din("w1f", [FE, D, 2 * FI])
    w2f_d = din("w2f", [FE, FI, D])
    sw1_d = din("sw1", [D, 2 * SI])
    sw2_d = din("sw2", [SI, D])
    posr_d = din("posr", [1, L])
    pmask_d = din("pmask", [128, 128])
    out = nc.dram_tensor("out", [D, TOK], F32, kind="ExternalOutput").ap()

    dbg_t = {}
    for name, shp in dict(
        xn=[D, TOK], q=[D, TOK], kf=[1024, 1024], vt=[1024, 1024],
        ao=[D, TOK], acc1=[D, TOK], hm=[D, TOK], accA=[D, TOK], mbr=[NH, TOK],
        tab=[128, 2048], swp=[128, 128], qs=[128, 512], swe=[128, 384],
        p1s=[128, 1024], p1t=[128, 16], p2e=[128, 1024],
        sh=[D, TOK], accF=[D, TOK],
    ).items():
        if name in dbg:
            dbg_t[name] = nc.dram_tensor("dbg_" + name, shp, F32, kind="ExternalOutput").ap()

    def SB(name, shape, dtype):
        return es.enter_context(nc.sbuf_tensor(name, shape, dtype))

    acc = SB("acc", [128, DC * TOK], F32)
    accB = acc.bitcast(BF16)
    xnb = SB("xnb", [128, DC * TOK], BF16)
    AR = SB("AR", [128, 49152], BF16)
    ARf = AR.bitcast(F32)
    STT = SB("STT", [128, 48], F32)
    GL = SB("GL", [128, 2048], F32)
    GTM = SB("GTM", [128, 256], F32)
    GF = SB("GF", [128, 512], F32)
    GFB = GF.bitcast(BF16)
    PM = SB("PM", [128, 128], F32)
    CT = SB("CT", [128, 512], F32)
    CTB = SB("CTB", [128, 16], BF16)
    CBC = GL[:, 1024:2048]
    ANW = SB("ANW", [128, DC], F32)
    MNW = SB("MNW", [128, DC], F32)
    GW = SB("GW", [128, DC * (AE + FE)], BF16)
    ITAB = GF.bitcast(I32)
    IOT = GF[:, 128:256]
    ONER = SB("ONER", [1, 128], F32)
    IFD = SB("IFD", [1, 128], F32)
    psb = [es.enter_context(nc.psum_tensor(f"ps{i}", [128, 512], F32)) for i in range(8)]

    IDENT = CT[:, 0:128]
    TRIL = CT[:, 128:256]
    TRIU = CT[:, 256:384]
    C_NEGHALF = CT[:, 384:385]
    C_EPS = CT[:, 385:386]
    C_INVD = CT[:, 386:387]
    C_LOGB = CT[:, 387:388]
    C_I2PI = CT[:, 388:389]
    C_N2PI = CT[:, 389:390]
    C_P2PI = CT[:, 393:394]
    C_PI = CT[:, 394:395]
    C_NPI = CT[:, 395:396]
    C_HALFPI = CT[:, 390:391]
    C_NEG1 = CT[:, 391:392]
    C_SCL = CT[:, 392:393]
    ONEC = CTB[:, 0:1]

    def sem(name, producer, dma=False):
        return Sem(es.enter_context(nc.semaphore(name)), name, producer, dma)

    s_t = sem("s_t", "tensor")
    s_v = sem("s_v", "vector")
    s_sc = sem("s_sc", "scalar")
    s_gp = sem("s_gp", "gpsimd")
    s_x = [sem(f"s_x{i}", "dma", True) for i in range(2)]
    s_cst = sem("s_cst", "dma", True)
    s_out = sem("s_out", "dma", True)
    dslot = [DSlot(P, sem(f"s_wr{i}", "dma", True)) for i in range(8)]

    ps_guard = [[] for _ in range(8)]

    def mm(ap, lhsT, rhs, start, stop, waits=(), bank=None):
        w = list(waits)
        if bank is not None and start:
            w = w + ps_guard[bank]
        P.emit("tensor", lambda e, a=ap, l=lhsT, r=rhs, st=start, sp=stop:
               e.matmul(a, l, r, start=st, stop=sp), waits=w)

    def t_ms():
        return P.ms("tensor", s_t)

    def v_ms():
        return P.ms("vector", s_v)

    def sc_ms():
        return P.ms("scalar", s_sc)

    V = lambda fn, waits=(), inc=None: P.emit("vector", fn, waits, inc)
    S = lambda fn, waits=(), inc=None: P.emit("scalar", fn, waits, inc)
    _vch = [[]]
    _sch = [[]]

    def VC(fn, waits=()):
        P.emit("vector", fn, waits=list(waits) + _vch[0], inc=(s_v, 1))
        _vch[0] = [(s_v, s_v.count)]
        return (s_v, s_v.count)

    def SCC(fn, waits=()):
        P.emit("scalar", fn, waits=list(waits) + _sch[0], inc=(s_sc, 1))
        _sch[0] = [(s_sc, s_sc.count)]
        return (s_sc, s_sc.count)
    T = lambda fn, waits=(), inc=None: P.emit("tensor", fn, waits, inc)
    SY = lambda fn, waits=(), inc=None: P.emit("sync", fn, waits, inc)

    SY(lambda e: e.dma_start(PM[:, :], pmask_d), inc=(s_cst, 16))
    SY(lambda e: e.dma_start(GL[0:1, 0:L], posr_d), inc=(s_cst, 16))
    SY(lambda e: e.dma_start(ANW[:, :], anw_d), inc=(s_cst, 16))
    SY(lambda e: e.dma_start(MNW[:, :], mnw_d), inc=(s_cst, 16))
    cst_ms = (s_cst, 64)
    gw_a = dslot[7].dma(GW[:, 0:DC * AE], kcm(gwa_d, 0, DC, AE, AE))
    gw_f = dslot[7].dma(GW[:, DC * AE:], kcm(gwf_d, 0, DC, FE, FE))

    P.emit("gpsimd", lambda e: e.iota(IOT, [[1, 128]], base=0, channel_multiplier=-1,
                                      allow_small_or_imprecise_dtypes=True), inc=(s_gp, 1))
    gp_iota = (s_gp, s_gp.count)
    VC(lambda e: e.tensor_single_scalar(IDENT, IOT, 0.0, ALU.is_equal), waits=[gp_iota])
    VC(lambda e: e.tensor_single_scalar(TRIL, IOT, 0.0, ALU.is_gt))
    VC(lambda e: e.tensor_scalar(TRIL, TRIL, NEG, None, ALU.mult))
    VC(lambda e: e.tensor_single_scalar(TRIU, IOT, 0.0, ALU.is_lt))
    VC(lambda e: e.tensor_scalar(TRIU, TRIU, NEG, None, ALU.mult))
    for apx, val in ((C_NEGHALF, -0.5), (C_EPS, EPS), (C_INVD, 1.0 / D),
                     (C_LOGB, -math.log(10000.0) / 64.0), (C_I2PI, 1.0 / (2 * math.pi)),
                     (C_N2PI, -2 * math.pi), (C_HALFPI, math.pi / 2.0), (C_NEG1, -1.0),
                     (C_P2PI, 2 * math.pi), (C_PI, math.pi), (C_NPI, -math.pi),
                     (C_SCL, SCL)):
        VC(lambda e, a=apx, v=val: e.memset(a, v))
    VC(lambda e: e.memset(ONER[:, :], 1.0))
    VC(lambda e: e.memset(STT[:, 32:48], 0.0))
    const_v = VC(lambda e: e.memset(ONEC, 1.0))

    O_Q = 0
    O_KF = 16384
    O_VT = 24576
    O_AO = 32768
    O_RING = 40960
    TB = 16384
    TSCR = 18432
    ACC_XT = 0
    ACC_H = 8192
    ACC_XB = 12288
    A_EXPS = 0
    A_PT = 512
    A_MBB = 1536

    arb = lambda c0, c1: AR[:, c0:c1]
    arf = lambda c0, c1: ARf[:, c0:c1]
    accf = lambda c0, c1: acc[:, c0:c1]
    accb = lambda c0, c1: accB[:, c0:c1]

    qsl = lambda hc, c0, n: arb(O_Q + hc * 1024 + c0, O_Q + hc * 1024 + c0 + n)
    kfsl = lambda g, c0, n: arb(O_KF + g * 2048 + c0, O_KF + g * 2048 + c0 + n)
    vtsl = lambda kb, c0, n: arb(O_VT + kb * 512 + c0, O_VT + kb * 512 + c0 + n)
    aosl = lambda hc, c0, n: arb(O_AO + hc * 1024 + c0, O_AO + hc * 1024 + c0 + n)

    def tile_tables(t):
        P.emit("gpsimd", lambda e: e.iota(GF[0:1, 256:320], [[1, 64]], base=0,
                                          channel_multiplier=0,
                                          allow_small_or_imprecise_dtypes=True),
               inc=(s_gp, 1))
        gi = (s_gp, s_gp.count)
        SCC(lambda e: e.activation(GF[0:1, 256:320], GF[0:1, 256:320], AF.Exp,
                                   bias=0.0, scale=C_LOGB[0:1, 0:1]),
            waits=[gi, const_v, cst_ms])
        SCC(lambda e: e.activation(IFD[0:1, 0:64], GF[0:1, 256:320], AF.Identity,
                                   bias=0.0, scale=1.0))
        if_ms = SCC(lambda e: e.activation(IFD[0:1, 64:128], GF[0:1, 256:320], AF.Identity,
                                           bias=0.0, scale=1.0))
        b = 6
        mm(psb[b][:, :], IFD[0:1, 0:128], GL[0:1, t * 512:(t + 1) * 512],
           True, True, waits=[if_ms, cst_ms], bank=b)
        ang_ms = t_ms()
        SA = arf(TSCR, TSCR + 512)
        SY_ = arf(TSCR + 512, TSCR + 1024)
        prev_sin = []
        for kind, dstc in (("cos", TB + 0), ("sin", TB + 512)):
            if kind == "cos":
                V(lambda e: e.tensor_scalar(SA, psb[b][:, :], C_HALFPI[:, 0:1], None, ALU.add),
                  waits=[ang_ms] + prev_sin)
            else:
                V(lambda e: e.tensor_scalar(SA, psb[b][:, :], 0.0, None, ALU.add),
                  waits=[ang_ms] + prev_sin)
            V(lambda e: e.tensor_scalar(SY_, SA, C_I2PI[:, 0:1], None, ALU.mult))
            V(lambda e: e.tensor_copy(ITAB[:, 0:512], SY_))
            V(lambda e: e.tensor_copy(SY_, ITAB[:, 0:512]))
            V(lambda e: e.scalar_tensor_tensor(SA, SY_, C_N2PI[:, 0:1], SA,
                                               ALU.mult, ALU.add))
            V(lambda e: e.tensor_scalar(SY_, SA, C_PI[:, 0:1], None, ALU.is_gt))
            V(lambda e: e.scalar_tensor_tensor(SA, SY_, C_N2PI[:, 0:1], SA,
                                               ALU.mult, ALU.add))
            V(lambda e: e.tensor_scalar(SY_, SA, C_NPI[:, 0:1], None, ALU.is_lt))
            V(lambda e: e.scalar_tensor_tensor(SA, SY_, C_P2PI[:, 0:1], SA,
                                               ALU.mult, ALU.add), inc=(s_v, 1))
            red = (s_v, s_v.count)
            S(lambda e, dstc=dstc: e.activation(arf(dstc, dstc + 512), SA, AF.Sin,
                                                bias=0.0, scale=1.0),
              waits=[red], inc=(s_sc, 1))
            prev_sin = [(s_sc, s_sc.count)]
        sins = (s_sc, s_sc.count)
        V(lambda e: e.tensor_scalar(ARf[0:64, TB + 1024:TB + 1536],
                                    ARf[0:64, TB + 512:TB + 1024],
                                    C_NEG1[0:64, 0:1], None, ALU.mult), waits=[sins])
        V(lambda e: e.tensor_copy(ARf[64:128, TB + 1024:TB + 1536],
                                  ARf[64:128, TB + 512:TB + 1024]), inc=(s_v, 1))
        tab_ms = (s_v, s_v.count)
        ps_guard[b] = [tab_ms]
        return tab_ms

    def rope_evac(psrc, psw, ctab, stab, dst, wlist, tmp=None):
        if norope == 1:
            V(lambda e: e.tensor_copy(dst, psrc), waits=wlist, inc=(s_v, 1))
            return (s_v, s_v.count)
        if norope == 2:
            V(lambda e: e.tensor_copy(dst, psw), waits=wlist, inc=(s_v, 1))
            return (s_v, s_v.count)
        if norope == 3:
            V(lambda e: e.tensor_copy(dst, tmp), waits=wlist, inc=(s_v, 1))
            return (s_v, s_v.count)
        T1 = arf(TSCR + 1024, TSCR + 1536)
        T2 = arf(TSCR + 1536, TSCR + 2048)
        V(lambda e: e.tensor_tensor(T1, psrc, ctab, ALU.mult), waits=wlist)
        V(lambda e: e.tensor_tensor(T2, psw, stab, ALU.mult))
        V(lambda e: e.tensor_tensor(dst, T1, T2, ALU.add), inc=(s_v, 1))
        return (s_v, s_v.count)

    def ln_tile(t, prev_done):
        own = t < 2
        xt = accf(ACC_XT, ACC_XT + 8192)
        xcol = lambda kc: accf(ACC_XT + kc * 512, ACC_XT + (kc + 1) * 512)
        SY(lambda e, t=t: e.dma_start(
            xt, kcm(xin, t * 512, DC, 512, L)),
            waits=prev_done, inc=(s_x[t % 2], 16))
        x_ms = (s_x[t % 2], 16 * (t // 2 + 1))
        xb = accb(2 * ACC_XB, 2 * ACC_XB + 8192)
        V(lambda e: e.tensor_copy(xb, xt), waits=[x_ms], inc=(s_v, 1))
        xb_ms = (s_v, s_v.count)
        b_sum, b_sq = 6, 7
        for kc in range(DC):
            mm(psb[b_sum][0:1, :], ONEC,
               accb(2 * ACC_XB + kc * 512, 2 * ACC_XB + (kc + 1) * 512),
               kc == 0, kc == DC - 1, waits=[xb_ms] if kc == 0 else (), bank=b_sum)
        sum_ms = t_ms()
        S(lambda e: e.activation(xb, xt, AF.Square, bias=0.0, scale=1.0),
          waits=[sum_ms], inc=(s_sc, 1))
        sq_ms = (s_sc, s_sc.count)
        for kc in range(DC):
            mm(psb[b_sq][0:1, :], ONEC,
               accb(2 * ACC_XB + kc * 512, 2 * ACC_XB + (kc + 1) * 512),
               kc == 0, kc == DC - 1, waits=[sq_ms] if kc == 0 else (), bank=b_sq)
        sqmm_ms = t_ms()
        RM = ARf[0:1, TSCR:TSCR + 512]
        RV = ARf[0:1, TSCR + 512:TSCR + 1024]
        RR = ARf[0:1, TSCR + 1024:TSCR + 1536]
        RMR = ARf[0:1, TSCR + 1536:TSCR + 2048]
        R2 = RM
        C0E = C_EPS[0:1, 0:1]
        C0H = C_NEGHALF[0:1, 0:1]
        VC(lambda e: e.tensor_scalar(RM, psb[b_sum][0:1, :], C_INVD[0:1, 0:1], None, ALU.mult),
           waits=[sum_ms])
        ps_guard[b_sum] = [(s_v, s_v.count)]
        VC(lambda e: e.tensor_scalar(RV, psb[b_sq][0:1, :], C_INVD[0:1, 0:1], None, ALU.mult),
           waits=[sqmm_ms])
        ps_guard[b_sq] = [(s_v, s_v.count)]
        VC(lambda e: e.tensor_tensor(RMR, RM, RM, ALU.mult))
        VC(lambda e: e.tensor_tensor(RV, RV, RMR, ALU.subtract))
        vr = VC(lambda e: e.tensor_scalar(RR, RV, C0E, None, ALU.add))
        SCC(lambda e: e.activation(RR, RR, AF.Ln, bias=0.0, scale=1.0), waits=[vr])
        r_ms = SCC(lambda e: e.activation(RR, RR, AF.Exp, bias=0.0, scale=C0H))
        rows_ms = VC(lambda e: e.tensor_tensor(RMR, RM, RR, ALU.mult), waits=[r_ms])
        VC(lambda e: e.tensor_scalar(R2, RV, C0E, None, ALU.add))
        VC(lambda e: e.reciprocal(R2, R2))
        VC(lambda e: e.tensor_tensor(R2, RV, R2, ALU.mult))
        r2v = VC(lambda e: e.tensor_scalar(R2, R2, C0E, None, ALU.add))
        SCC(lambda e: e.activation(R2, R2, AF.Ln, bias=0.0, scale=1.0), waits=[r2v])
        r2_ms = SCC(lambda e: e.activation(R2, R2, AF.Exp, bias=0.0, scale=C0H))
        mm(psb[4][:, :], ONER[0:1, 0:128], RR, True, True, waits=[rows_ms, r_ms], bank=4)
        mm(psb[5][:, :], ONER[0:1, 0:128], RMR, True, True, waits=[rows_ms], bank=5)
        bc_ms = t_ms()
        mm(psb[6][:, :], ONER[0:1, 0:128], R2, True, True, waits=[r2_ms], bank=6)
        bc2_ms = t_ms()
        for kc in range(DC):
            if own:
                xdst = xnb[:, kc * 1024 + t * 512: kc * 1024 + (t + 1) * 512]
            else:
                xdst = accb(2 * ACC_XB + kc * 512, 2 * ACC_XB + (kc + 1) * 512)
            hdst = accb(2 * ACC_H + kc * 512, 2 * ACC_H + (kc + 1) * 512)
            V(lambda e, kc=kc, xdst=xdst: e.tensor_tensor(xdst, xcol(kc), psb[4][:, :],
                                                          ALU.mult),
              waits=[bc_ms] if kc == 0 else ())
            V(lambda e, xdst=xdst: e.tensor_tensor(xdst, xdst, psb[5][:, :], ALU.subtract))
            V(lambda e, xdst=xdst, hdst=hdst: e.tensor_tensor(hdst, xdst, psb[6][:, :],
                                                              ALU.mult),
              waits=[bc2_ms] if kc == 0 else ())
            V(lambda e, kc=kc, hdst=hdst: e.tensor_scalar(hdst, hdst, ANW[:, kc:kc + 1],
                                                          None, ALU.mult))
        h_ms = v_ms()
        ps_guard[4] = [h_ms]
        ps_guard[5] = [h_ms]
        ps_guard[6] = [h_ms]
        return h_ms

    hcol = lambda kc: accb(2 * ACC_H + kc * 512, 2 * ACC_H + (kc + 1) * 512)
    hblk = lambda kc, tb: accB[:, 2 * ACC_H + kc * 512 + tb * 128:
                               2 * ACC_H + kc * 512 + tb * 128 + 128]

    def rslot(i):
        return arb(O_RING + i * 4096, O_RING + (i + 1) * 4096)

    prev_done = []
    for t in range(4):
        own = t < 2
        tab_ms = tile_tables(t)
        h_ms = ln_tile(t, prev_done)
        CTAB = arf(TB, TB + 512)
        STAB = arf(TB + 1024, TB + 1536)
        for kc in range(DC):
            sl = kc % 2
            slab = rslot(sl)[:, 0:1024]
            slab_ms = dslot[sl].dma(slab, wk_d[kc * 128:(kc + 1) * 128, :])
            for dc in range(NKV):
                mm(psb[dc][:, 0:512], slab[:, dc * 256:dc * 256 + 128], hcol(kc),
                   kc == 0, kc == DC - 1, waits=[h_ms, slab_ms], bank=dc)
                mm(psb[4 + dc][:, 0:512], slab[:, dc * 256 + 128:dc * 256 + 256], hcol(kc),
                   kc == 0, kc == DC - 1, waits=[h_ms, slab_ms], bank=4 + dc)
            dslot[sl].guard = [t_ms()]
        kmm_ms = t_ms()
        for dc in range(NKV):
            r2 = rope_evac(psb[dc][:, :], psb[4 + dc][:, :], CTAB, STAB,
                           kfsl(dc, t * 512, 512), [kmm_ms, tab_ms])
            ps_guard[dc] = [r2]
            ps_guard[4 + dc] = [r2]
        for kc in range(DC):
            sl = kc % 2
            slab = rslot(sl)[:, 0:512]
            slab_ms = dslot[sl].dma(slab, wv_d[kc * 128:(kc + 1) * 128, :])
            for tb in range(4):
                mm(psb[tb][:, 0:512], hblk(kc, tb), slab,
                   kc == 0, kc == DC - 1, waits=[h_ms, slab_ms], bank=tb)
            dslot[sl].guard = [t_ms()]
        vmm_ms = t_ms()
        for tb in range(4):
            kb = t * 4 + tb
            V(lambda e, tb=tb, kb=kb: e.tensor_copy(vtsl(kb, 0, 512), psb[tb][:, :]),
              waits=[vmm_ms], inc=(s_v, 1))
            ps_guard[tb] = [(s_v, s_v.count)]
        if own:
            QC = arf(TB + 1536, TB + 2048)
            V(lambda e: e.tensor_scalar(QC, CTAB, C_SCL[:, 0:1], None, ALU.mult),
              waits=[tab_ms])
            QS = arf(TSCR + 512, TSCR + 1024)
            V(lambda e: e.tensor_scalar(QS, STAB, C_SCL[:, 0:1], None, ALU.mult),
              inc=(s_v, 1))
            qtab_ms = (s_v, s_v.count)
            if t == 0 and "tab" in dbg_t:
                SY(lambda e: e.dma_start(dbg_t["tab"], arf(TB, TB + 2048)),
                   waits=[qtab_ms], inc=(s_out, 16))
                SY(lambda e: e.dma_start(dbg_t["qs"], arf(TSCR + 512, TSCR + 1024)),
                   waits=[qtab_ms], inc=(s_out, 16))
            for hc in range(NH):
                sl = hc % 2
                slab = rslot(sl)
                slab_ms = dslot[sl].dma(slab, kcm(wq_d, hc * 256, DC, 256, 2 * D))
                for kc in range(DC):
                    mm(psb[5][:, 0:512], slab[:, kc * 256:kc * 256 + 128], hcol(kc),
                       kc == 0, kc == DC - 1, waits=[h_ms, slab_ms], bank=5)
                for kc in range(DC):
                    mm(psb[6][:, 0:512], slab[:, kc * 256 + 128:kc * 256 + 256], hcol(kc),
                       kc == 0, kc == DC - 1, waits=[h_ms, slab_ms], bank=6)
                qmm = t_ms()
                dslot[sl].guard = [qmm]
                qr = rope_evac(psb[5][:, :], psb[6][:, :], QC, QS,
                               qsl(hc, t * 512, 512), [qmm, qtab_ms])
                ps_guard[5] = [qr]
                ps_guard[6] = [qr]
        prev_done = [v_ms(), sc_ms()]

    p1_v = v_ms()

    def dump_fm(name, src_fn, nchunk, ms):
        if name not in dbg_t:
            return
        for kc in range(nchunk):
            for c2 in range(2):
                V(lambda e, kc=kc, c2=c2: e.tensor_copy(
                    GF[:, 0:512] if False else GF[:, 0:512],
                    src_fn(kc)[:, c2 * 512:(c2 + 1) * 512]), waits=[ms], inc=(s_v, 1))
                m = (s_v, s_v.count)
                SY(lambda e, kc=kc, c2=c2, name=name: e.dma_start(
                    dbg_t[name][kc * 128:(kc + 1) * 128, c2 * 512:(c2 + 1) * 512],
                    GF[:, 0:512]), waits=[m], inc=(s_out, 16))
                V(lambda e: e.tensor_copy(STT[0:1, 32:33], STT[0:1, 32:33]),
                  waits=[(s_out, s_out.count)])

    dump_fm("xn", lambda kc: xnb[:, kc * 1024:(kc + 1) * 1024], DC, p1_v)
    dump_fm("q", lambda kc: arb(O_Q + kc * 1024, O_Q + (kc + 1) * 1024), NH, p1_v)
    dump_fm("kf", lambda kc: arb(O_KF + kc * 1024, O_KF + (kc + 1) * 1024), NKV * 2, p1_v)
    if "kf" in dbg_t:
        pass
    dump_fm("vt", lambda kc: arb(O_VT + kc * 1024, O_VT + (kc + 1) * 1024), 8, p1_v)

    def pass1(h, hbuf):
        g = h // 4
        prev_mbt = []
        p1g = {0: [], 1: []}
        exp_guard = {0: [], 1: []}
        for j in range(8):
            cb = 16 * (j % 2)
            C = lambda i: STT[:, cb + i: cb + i + 1]
            pieces = []
            nk = (j + 1) * 128
            for seg in range(2):
                off = 0
                while off < nk:
                    wp = min(512, nk - off)
                    bank = 4 + (len(pieces) % 4)
                    mm(psb[bank][:, 0:wp], qsl(h, j * 128, 128),
                       kfsl(g, seg * 1024 + off, wp), True, True,
                       waits=[p1_v], bank=bank)
                    wl = [t_ms()]
                    dlo = j * 128
                    if off <= dlo < off + wp:
                        mt = TRIL if seg == 0 else PM[:, :]
                        V(lambda e, bank=bank, o=dlo - off, mt=mt: e.tensor_tensor(
                            psb[bank][:, o:o + 128], psb[bank][:, o:o + 128], mt, ALU.add),
                          waits=wl + [const_v, cst_ms], inc=(s_v, 1))
                        wl = [(s_v, s_v.count)]
                    pi = len(pieces)
                    VC(lambda e, bank=bank, pi=pi, wp=wp, cb=cb: e.tensor_reduce(
                        STT[:, cb + pi:cb + pi + 1], psb[bank][:, 0:wp], AX.X, ALU.max),
                       waits=wl + p1g[j % 2])
                    pieces.append((bank, wp, (s_v, s_v.count)))
                    off += wp
            npc = len(pieces)
            VC(lambda e, cb=cb: e.tensor_copy(STT[:, cb+8:cb+9], STT[:, cb:cb+1]),
               waits=[pieces[0][2]])
            for pi in range(1, npc):
                VC(lambda e, pi=pi, cb=cb: e.tensor_tensor(STT[:, cb+8:cb+9],
                   STT[:, cb+8:cb+9], STT[:, cb+pi:cb+pi+1], ALU.max),
                   waits=[pieces[pi][2]])
            negM = VC(lambda e, cb=cb: e.tensor_scalar(STT[:, cb+9:cb+10],
                      STT[:, cb+8:cb+9], C_NEG1[:, 0:1], None, ALU.mult))
            for pi, (bank, wp, _) in enumerate(pieces):
                eb = pi % 2
                S(lambda e, bank=bank, wp=wp, pi=pi, cb=cb, eb=eb: e.activation(
                    accb(2 * A_EXPS + eb * 512, 2 * A_EXPS + eb * 512 + wp),
                    psb[bank][:, 0:wp], AF.Exp,
                    bias=STT[:, cb+9:cb+10], scale=1.0),
                  waits=[negM] + exp_guard[eb], inc=(s_sc, 1))
                ps_guard[bank] = [(s_sc, s_sc.count)]
                VC(lambda e, wp=wp, pi=pi, cb=cb, eb=eb: e.tensor_reduce(
                    STT[:, cb + 4 + pi:cb + 5 + pi],
                    accb(2 * A_EXPS + eb * 512, 2 * A_EXPS + eb * 512 + wp),
                    AX.X, ALU.add),
                   waits=[(s_sc, s_sc.count)])
                exp_guard[eb] = [(s_v, s_v.count)]
            VC(lambda e, cb=cb: e.tensor_copy(STT[:, cb+10:cb+11], STT[:, cb+4:cb+5]))
            for pi in range(1, npc):
                VC(lambda e, pi=pi, cb=cb: e.tensor_tensor(STT[:, cb+10:cb+11],
                   STT[:, cb+10:cb+11], STT[:, cb+4+pi:cb+5+pi], ALU.add))
            st_ms = VC(lambda e, cb=cb: e.tensor_copy(STT[:, cb+12:cb+13],
                       STT[:, cb+10:cb+11]), waits=list(prev_mbt))
            SCC(lambda e, cb=cb: e.activation(STT[:, cb+12:cb+13], STT[:, cb+12:cb+13],
                                              AF.Ln, bias=0.0, scale=1.0), waits=[st_ms])
            mb_ms = SCC(lambda e, cb=cb: e.activation(STT[:, cb+12:cb+13],
                        STT[:, cb+12:cb+13], AF.Identity, bias=STT[:, cb+8:cb+9],
                        scale=1.0))
            p1g[j % 2] = [mb_ms]
            if h == 4 and j == 1 and "p1t" in dbg_t:
                SY(lambda e: e.dma_start(dbg_t["p1t"], STT[:, 0:16]),
                   waits=[mb_ms], inc=(s_out, 16))
            tb = 4 + (j % 4)
            T(lambda e, tb=tb, cb=cb: e.transpose(psb[tb][0:1, 0:128], STT[:, cb+12:cb+13], IDENT),
              waits=[mb_ms] + ps_guard[tb])
            tms = t_ms()
            prev_mbt = [tms]
            V(lambda e, tb=tb, hbuf=hbuf, j=j: e.tensor_copy(
                GL[0:1, hbuf * 1024 + j * 128: hbuf * 1024 + (j + 1) * 128],
                psb[tb][0:1, 0:128]), waits=[tms], inc=(s_v, 1))
            ps_guard[tb] = [(s_v, s_v.count)]
        if "mbr" in dbg_t:
            mm_ = v_ms()
            SY(lambda e, h=h, hbuf=hbuf: e.dma_start(
                dbg_t["mbr"][h:h + 1, :], GL[0:1, hbuf * 1024:(hbuf + 1) * 1024]),
               waits=[mm_], inc=(s_out, 16))
        return v_ms()

    pt_guard = [[], []]

    def pass2(h, hbuf, mb_ready):
        g = h // 4
        mbb0 = A_MBB + hbuf * 1024
        for c2 in range(2):
            mm(psb[c2][:, :], ONER[0:1, 0:128],
               GL[0:1, hbuf * 1024 + c2 * 512: hbuf * 1024 + (c2 + 1) * 512],
               True, True, waits=[mb_ready], bank=c2)
            bmms = t_ms()
            V(lambda e, c2=c2: e.tensor_copy(accf(mbb0 + c2 * 512, mbb0 + (c2 + 1) * 512),
                                             psb[c2][:, :]), waits=[bmms], inc=(s_v, 1))
            ps_guard[c2] = [(s_v, s_v.count)]
        mbb_ms = (s_v, s_v.count)
        contrib = {0: [], 1: []}
        for kc in range(16):
            qlo = (kc % 8) * 128
            for qt in range(2):
                if qlo < (qt + 1) * 512:
                    contrib[qt].append(kc)
        for kc in range(16):
            seg, m8 = kc // 8, kc % 8
            qlo = m8 * 128
            ptb = kc % 2
            ptbuf = accB[:, 2 * A_PT + ptb * 1024: 2 * A_PT + ptb * 1024 + 1024]
            exp_parts = []
            off = qlo
            while off < 1024:
                wt = min(512, 1024 - off)
                bank = kc % 2
                mm(psb[bank][:, 0:wt], kfsl(g, seg * 1024 + m8 * 128, 128),
                   qsl(h, off, wt), True, True, waits=[p1_v], bank=bank)
                wl = [t_ms(), mbb_ms]
                if off == qlo:
                    mt = TRIU if seg == 0 else PM[:, :]
                    V(lambda e, bank=bank, mt=mt: e.tensor_tensor(
                        psb[bank][:, 0:128], psb[bank][:, 0:128], mt, ALU.add),
                      waits=wl, inc=(s_v, 1))
                    wl = [(s_v, s_v.count)]
                V(lambda e, bank=bank, wt=wt, off=off: e.tensor_tensor(
                    psb[bank][:, 0:wt], psb[bank][:, 0:wt],
                    accf(mbb0 + off, mbb0 + off + wt), ALU.subtract),
                  waits=wl, inc=(s_v, 1))
                sub_ms = (s_v, s_v.count)
                S(lambda e, bank=bank, wt=wt, off=off, qlo=qlo, ptbuf=ptbuf: e.activation(
                    ptbuf[:, off - qlo:off - qlo + wt], psb[bank][:, 0:wt], AF.Exp,
                    bias=0.0, scale=1.0),
                  waits=[sub_ms] + pt_guard[ptb], inc=(s_sc, 1))
                ps_guard[bank] = [(s_sc, s_sc.count)]
                exp_parts.append((s_sc, s_sc.count))
                off += wt
            for qt in range(2):
                q0 = qt * 512
                if kc not in contrib[qt]:
                    continue
                lo = max(q0, qlo)
                wt = q0 + 512 - lo
                bank = 2 + qt
                first = kc == contrib[qt][0]
                last = kc == contrib[qt][-1]
                wl = list(exp_parts)
                if first:
                    wl += ps_guard[bank]
                mm(psb[bank][:, lo - q0:lo - q0 + wt], vtsl(kc, g * 128, 128),
                   ptbuf[:, lo - qlo:lo - qlo + wt], first, last, waits=wl)
            pt_guard[ptb] = [t_ms()]
        pv_ms = t_ms()
        for qt in range(2):
            V(lambda e, qt=qt: e.tensor_copy(aosl(h, qt * 512, 512), psb[2 + qt][:, :]),
              waits=[pv_ms], inc=(s_v, 1))
            ps_guard[2 + qt] = [(s_v, s_v.count)]
        return v_ms()

    mb_done = {}
    for h in range(NH):
        hbuf = h % 2
        mb_done[h] = pass1(h, hbuf)
        if h > 0:
            pass2(h - 1, (h - 1) % 2, mb_done[h - 1])
    pass2(NH - 1, (NH - 1) % 2, mb_done[NH - 1])
    ao_ms = v_ms()

    dump_fm("ao", lambda kc: arb(O_AO + kc * 1024, O_AO + (kc + 1) * 1024), DC, ao_ms)

    attn_end_t = t_ms()
    for i in range(4):
        dslot[i].guard = [attn_end_t]
    def rslot2(i):
        return arb(i * 2048, (i + 1) * 2048)
    for hc in range(DC):
        sl = hc % 4
        slab = rslot2(sl)
        slab_ms = dslot[sl].dma(slab, kcm(wo_d, hc * 128, DC, 128, D))
        for tt in range(2):
            bank = tt
            for kc in range(DC):
                mm(psb[bank][:, :], slab[:, kc * 128:(kc + 1) * 128],
                   aosl(kc, tt * 512, 512), kc == 0, kc == DC - 1,
                   waits=[ao_ms, slab_ms], bank=bank)
            womm = t_ms()
            V(lambda e, hc=hc, tt=tt, bank=bank: e.tensor_tensor(
                accf(hc * 1024 + tt * 512, hc * 1024 + tt * 512 + 512), psb[bank][:, :],
                xnb[:, hc * 1024 + tt * 512: hc * 1024 + tt * 512 + 512], ALU.add),
              waits=[womm], inc=(s_v, 1))
            ps_guard[bank] = [(s_v, s_v.count)]
        dslot[sl].guard = [t_ms()]
    acc1_ms = v_ms()

    def dump_acc(name, ms):
        if name not in dbg_t:
            return
        for kc in range(DC):
            SY(lambda e, kc=kc, name=name: e.dma_start(
                dbg_t[name][kc * 128:(kc + 1) * 128, :], accf(kc * 1024, (kc + 1) * 1024)),
                waits=[ms], inc=(s_out, 16))

    dump_acc("acc1", acc1_ms)

    O_HM = 0
    O_ACTA = 16384
    O_RA = 24576
    O_W2A = 32768
    for tt in range(2):
        b_s = 6
        for kc in range(DC):
            sqw = [acc1_ms] if kc == 0 else [t_ms()]
            S(lambda e, kc=kc, tt=tt: e.activation(
                GFB[:, 0:512], accf(kc * 1024 + tt * 512, kc * 1024 + tt * 512 + 512),
                AF.Square, bias=0.0, scale=1.0), waits=sqw, inc=(s_sc, 1))
            sqm = (s_sc, s_sc.count)
            mm(psb[b_s][0:1, :], ONEC, GFB[:, 0:512], kc == 0, kc == DC - 1,
               waits=[sqm], bank=b_s)
        smm = t_ms()
        RO = GL[0:1, 0:512]
        V(lambda e, b_s=b_s: e.tensor_scalar(RO, psb[b_s][0:1, :], C_INVD[0:1, 0:1],
                                             None, ALU.mult), waits=[smm], inc=(s_v, 1))
        ps_guard[b_s] = [(s_v, s_v.count)]
        V(lambda e: e.tensor_scalar(RO, RO, C_EPS[0:1, 0:1], None, ALU.add), inc=(s_v, 1))
        rv = (s_v, s_v.count)
        S(lambda e: e.activation(RO, RO, AF.Ln, bias=0.0, scale=1.0), waits=[rv])
        S(lambda e: e.activation(RO, RO, AF.Exp, bias=0.0, scale=C_NEGHALF[0:1, 0:1]),
          inc=(s_sc, 1))
        rr = (s_sc, s_sc.count)
        mm(psb[7][:, :], ONER[0:1, 0:128], RO, True, True, waits=[rr], bank=7)
        bc = t_ms()
        for kc in range(DC):
            dst = arb(O_HM + kc * 1024 + tt * 512, O_HM + kc * 1024 + tt * 512 + 512)
            V(lambda e, kc=kc, tt=tt, dst=dst: e.tensor_tensor(
                dst, accf(kc * 1024 + tt * 512, kc * 1024 + tt * 512 + 512),
                psb[7][:, :], ALU.mult), waits=[bc] if kc == 0 else ())
            V(lambda e, kc=kc, dst=dst: e.tensor_scalar(dst, dst, MNW[:, kc:kc + 1],
                                                        None, ALU.mult))
        hm_t = v_ms()
        ps_guard[7] = [hm_t]
    hm_ms = v_ms()
    dump_fm("hm", lambda kc: arb(O_HM + kc * 1024, O_HM + (kc + 1) * 1024), DC, hm_ms)

    for src, gofs, ne, dst0 in ((0, 0, AE, 0), (1, DC * AE, FE, 1024)):
        for tt in range(2):
            bank = 6
            for kc in range(DC):
                rhs = (arb(O_HM + kc * 1024 + tt * 512, O_HM + kc * 1024 + tt * 512 + 512)
                       if src == 0 else
                       xnb[:, kc * 1024 + tt * 512: kc * 1024 + tt * 512 + 512])
                mm(psb[bank][0:ne, :], GW[:, gofs + kc * ne: gofs + (kc + 1) * ne], rhs,
                   kc == 0, kc == DC - 1,
                   waits=[hm_ms, gw_a, gw_f], bank=bank)
            gmm = t_ms()
            V(lambda e, ne=ne, dst0=dst0, tt=tt, bank=bank: e.tensor_copy(
                GL[0:ne, dst0 + tt * 512: dst0 + tt * 512 + 512], psb[bank][0:ne, :]),
              waits=[gmm], inc=(s_v, 1))
            ps_guard[bank] = [(s_v, s_v.count)]
    glog_ms = v_ms()

    for tile in range(8):
        bank = 6
        T(lambda e, tile=tile, bank=bank: e.transpose(
            psb[bank][0:128, 0:8], GL[0:8, tile * 128:(tile + 1) * 128], IDENT[0:8, 0:8]),
          waits=[glog_ms] + ps_guard[bank])
        tm1 = t_ms()
        lg = GTM[:, tile * 8:(tile + 1) * 8]
        VC(lambda e, lg=lg, bank=bank: e.tensor_copy(lg, psb[bank][0:128, 0:8]),
           waits=[tm1])
        ps_guard[bank] = [(s_v, s_v.count)]
        mneg = VC(lambda e, lg=lg: e.tensor_reduce(STT[:, 16:17], lg, AX.X, ALU.max,
                                                   negate=True))
        e_ms = SCC(lambda e, lg=lg: e.activation(lg, lg, AF.Exp, bias=STT[:, 16:17],
                                                 scale=1.0), waits=[mneg])
        cmb = GTM[:, 64 + tile * 8: 64 + (tile + 1) * 8]
        VC(lambda e, lg=lg: e.tensor_reduce(STT[:, 17:18], lg, AX.X, ALU.max),
           waits=[e_ms])
        VC(lambda e, lg=lg, cmb=cmb: e.tensor_scalar(cmb, lg, STT[:, 17:18], None,
                                                     ALU.is_equal))
        VC(lambda e, cmb=cmb: e.tensor_scalar(GF[:, 256:264], cmb, -1.0, 1.0,
                                              ALU.mult, ALU.add))
        VC(lambda e, lg=lg: e.tensor_tensor(GF[:, 256:264], lg, GF[:, 256:264], ALU.mult))
        VC(lambda e: e.tensor_reduce(STT[:, 18:19], GF[:, 256:264], AX.X, ALU.max))
        VC(lambda e: e.tensor_scalar(GF[:, 264:272], GF[:, 256:264], STT[:, 18:19],
                                     None, ALU.is_equal))
        VC(lambda e, cmb=cmb: e.tensor_tensor(cmb, cmb, GF[:, 264:272], ALU.add))
        VC(lambda e: e.tensor_tensor(STT[:, 17:18], STT[:, 17:18], STT[:, 18:19], ALU.add))
        VC(lambda e: e.reciprocal(STT[:, 17:18], STT[:, 17:18]))
        VC(lambda e, lg=lg, cmb=cmb: e.tensor_tensor(cmb, cmb, lg, ALU.mult))
        VC(lambda e, cmb=cmb: e.tensor_scalar(cmb, cmb, STT[:, 17:18], None, ALU.mult))
        T(lambda e, tile=tile, bank=bank: e.transpose(
            psb[bank][0:128, 0:16], GL[0:16, 1024 + tile * 128:1024 + (tile + 1) * 128],
            IDENT[0:16, 0:16]), waits=[glog_ms] + ps_guard[bank])
        tm2 = t_ms()
        VC(lambda e, bank=bank: e.tensor_copy(GF[:, 0:16], psb[bank][0:128, 0:16]),
           waits=[tm2])
        ps_guard[bank] = [(s_v, s_v.count)]
        fneg = VC(lambda e: e.tensor_reduce(STT[:, 20:21], GF[:, 0:16], AX.X, ALU.max,
                                            negate=True))
        fe_ms = SCC(lambda e: e.activation(GF[:, 0:16], GF[:, 0:16], AF.Exp,
                                           bias=STT[:, 20:21], scale=1.0), waits=[fneg])
        VC(lambda e: e.tensor_reduce(STT[:, 21:22], GF[:, 0:16], AX.X, ALU.add),
           waits=[fe_ms])
        VC(lambda e: e.reciprocal(STT[:, 22:23], STT[:, 21:22]))
        VC(lambda e: e.tensor_scalar(GF[:, 0:16], GF[:, 0:16], STT[:, 22:23], None, ALU.mult))
        VC(lambda e: e.tensor_copy(GF[:, 256:272], GF[:, 0:16]))
        VC(lambda e: e.memset(GF[:, 272:288], 0.0))
        for it in range(6):
            VC(lambda e: e.tensor_reduce(STT[:, 23:24], GF[:, 256:272], AX.X, ALU.max))
            VC(lambda e: e.tensor_scalar(GF[:, 288:304], GF[:, 256:272], STT[:, 23:24],
                                         None, ALU.is_equal))
            VC(lambda e: e.tensor_tensor(GF[:, 272:288], GF[:, 272:288], GF[:, 288:304],
                                         ALU.add))
            if it < 5:
                VC(lambda e: e.tensor_scalar(GF[:, 288:304], GF[:, 288:304], -1.0, 1.0,
                                             ALU.mult, ALU.add))
                VC(lambda e: e.tensor_tensor(GF[:, 256:272], GF[:, 256:272], GF[:, 288:304],
                                             ALU.mult))
        VC(lambda e, tile=tile: e.tensor_tensor(
            GTM[:, 128 + tile * 16:128 + (tile + 1) * 16], GF[:, 0:16], GF[:, 272:288],
            ALU.mult))
    gates_ms = v_ms()

    def cbcast(col_fn, cslot, waits):
        for tile in range(8):
            T(lambda e, tile=tile: e.transpose(psb[7][0:1, 0:128], col_fn(tile), IDENT),
              waits=list(waits) + ps_guard[7])
            tms = t_ms()
            V(lambda e, tile=tile: e.tensor_copy(
                GL[0:1, tile * 128:(tile + 1) * 128], psb[7][0:1, 0:128]),
              waits=[tms], inc=(s_v, 1))
            ps_guard[7] = [(s_v, s_v.count)]
        row_ms = v_ms()
        for c2 in range(2):
            mm(psb[7][:, :], ONER[0:1, 0:128], GL[0:1, c2 * 512:(c2 + 1) * 512],
               True, True, waits=[row_ms], bank=7)
            bm = t_ms()
            V(lambda e, c2=c2: e.tensor_copy(
                CBC[:, c2 * 512: c2 * 512 + 512],
                psb[7][:, :]), waits=[bm], inc=(s_v, 1))
            ps_guard[7] = [(s_v, s_v.count)]
        return v_ms()

    def gu_pairs(npairs, slab_fn, rhs_fn, act_base, cslot, extra):
        for pair in range(npairs):
            slab, slab_ms, dsl = slab_fn(pair)
            for tt in range(2):
                bg = (pair * 2 + tt) % 2
                bu = 2 + (pair * 2 + tt) % 2
                for kc in range(DC):
                    mm(psb[bg][:, :], slab[:, kc * 256: kc * 256 + 128], rhs_fn(kc, tt),
                       kc == 0, kc == DC - 1,
                       waits=([slab_ms] + list(extra)) if kc == 0 else (), bank=bg)
                for kc in range(DC):
                    mm(psb[bu][:, :], slab[:, kc * 256 + 128: kc * 256 + 256],
                       rhs_fn(kc, tt), kc == 0, kc == DC - 1,
                       waits=([slab_ms] + list(extra)) if kc == 0 else (), bank=bu)
                gu_mm = t_ms()
                sg = GFB[:, tt * 512:(tt + 1) * 512]
                S(lambda e, bg=bg, sg=sg: e.activation(sg, psb[bg][:, :], AF.Silu,
                                                       bias=0.0, scale=1.0),
                  waits=[gu_mm], inc=(s_sc, 1))
                sg_ms = (s_sc, s_sc.count)
                ps_guard[bg] = [sg_ms]
                dst = arb(act_base + pair * 1024 + tt * 512,
                          act_base + pair * 1024 + tt * 512 + 512)
                V(lambda e, bu=bu, sg=sg, dst=dst: e.tensor_tensor(
                    dst, psb[bu][:, :], sg, ALU.mult), waits=[sg_ms], inc=(s_v, 1))
                ps_guard[bu] = [(s_v, s_v.count)]
                if cslot is not None:
                    V(lambda e, dst=dst, tt=tt: e.tensor_tensor(
                        dst, dst, CBC[:, tt * 512: tt * 512 + 512], ALU.mult))
            dsl.guard = [t_ms()]
        return t_ms(), v_ms()

    def w2_stage(nic, w2base, act_base, extra):
        for hc in range(DC):
            for tt in range(2):
                bank = 4 + (hc * 2 + tt) % 2
                for ic in range(nic):
                    mm(psb[bank][:, :],
                       arb(w2base + ic * 2048 + hc * 128, w2base + ic * 2048 + hc * 128 + 128),
                       arb(act_base + ic * 1024 + tt * 512,
                           act_base + ic * 1024 + tt * 512 + 512),
                       ic == 0, ic == nic - 1, waits=list(extra) if ic == 0 else (),
                       bank=bank)
                wmm = t_ms()
                V(lambda e, hc=hc, tt=tt, bank=bank: e.tensor_tensor(
                    accf(hc * 1024 + tt * 512, hc * 1024 + tt * 512 + 512),
                    psb[bank][:, :],
                    accf(hc * 1024 + tt * 512, hc * 1024 + tt * 512 + 512), ALU.add),
                  waits=[wmm], inc=(s_v, 1))
                ps_guard[bank] = [(s_v, s_v.count)]
        return v_ms()

    wo_end_t = t_ms()
    for i in (4, 5, 6):
        dslot[i].guard = [wo_end_t]
    hm_rhs = lambda kc, tt: arb(O_HM + kc * 1024 + tt * 512, O_HM + kc * 1024 + tt * 512 + 512)
    for e_ in range(AE):
        cslot = e_ % 2
        cb_ms = cbcast(lambda tile, e_=e_: GTM[:, 64 + tile * 8 + e_: 64 + tile * 8 + e_ + 1],
                       cslot, [gates_ms])

        def slab_a(pair, e_=e_):
            sl = 4 + pair % 2
            dst = arb(O_RA + (pair % 2) * 4096, O_RA + (pair % 2) * 4096 + 4096)
            msd = dslot[sl].dma(dst, kcm(w13a_d, e_ * D * 2 * AI + pair * 256,
                                         DC, 256, 2 * AI))
            return dst, msd, dslot[sl]

        w2a_sl = dslot[6]
        w2a_holder = []

        def slab_a2(pair, e_=e_):
            r = slab_a(pair, e_)
            if pair == 1:
                w2a_holder.append(w2a_sl.dma(arb(O_W2A, O_W2A + 16384),
                                             kcm(w2a_d, e_ * AI * D, AIC, D, D)))
            return r

        lt, lv = gu_pairs(AIC, slab_a2, hm_rhs, O_ACTA, cslot, [cb_ms, hm_ms])
        w2a_ms = w2a_holder[0]
        vv = w2_stage(AIC, O_W2A, O_ACTA, [w2a_ms, lv])
        w2a_sl.guard = [t_ms()]
    accA_ms = v_ms()
    dump_acc("accA", accA_ms)

    moea_end_t = t_ms()
    dslot[4].guard = [moea_end_t]
    dslot[5].guard = [moea_end_t]
    O_ACTS = 0
    O_SW1 = 22528
    O_SW2 = 26624
    xn_rhs = lambda kc, tt: xnb[:, kc * 1024 + tt * 512: kc * 1024 + tt * 512 + 512]

    def slab_s(pair):
        dst = arb(O_SW1, O_SW1 + 4096)
        msd = dslot[4].dma(dst, kcm(sw1_d, pair * 256, DC, 256, 2 * SI))
        return dst, msd, dslot[4]

    lt, acts_v = gu_pairs(SIC, slab_s, xn_rhs, O_ACTS, None, [accA_ms])
    sw2_sl = dslot[5]
    for half in range(2):
        sw2_ms = sw2_sl.dma(arb(O_SW2, O_SW2 + 22528),
                            kcm(sw2_d, half * FI * D, FIC, D, D))
        vv = w2_stage(FIC, O_SW2, O_ACTS + half * FIC * 1024, [sw2_ms, acts_v])
        sw2_sl.guard = [t_ms()]
    sh_ms = v_ms()
    dump_acc("sh", sh_ms)

    sh_end_t = t_ms()
    for i in (1, 2, 3):
        dslot[i].guard = [sh_end_t]
    O_ACTF = 0
    O_RF = 11264
    O_W2F = 19456
    for e_ in range(FE):
        cslot = e_ % 2
        cb_ms = cbcast(
            lambda tile, e_=e_: GTM[:, 128 + tile * 16 + e_: 128 + tile * 16 + e_ + 1],
            cslot, [gates_ms])

        def slab_f(pair, e_=e_):
            sl = 2 + pair % 2
            dst = arb(O_RF + (pair % 2) * 4096, O_RF + (pair % 2) * 4096 + 4096)
            msd = dslot[sl].dma(dst, kcm(w1f_d, e_ * D * 2 * FI + pair * 256,
                                         DC, 256, 2 * FI))
            return dst, msd, dslot[sl]

        w2f_sl = dslot[1]
        w2f_holder = []

        def slab_f2(pair, e_=e_):
            r = slab_f(pair, e_)
            if pair == 1:
                w2f_holder.append(w2f_sl.dma(arb(O_W2F, O_W2F + 22528),
                                             kcm(w2f_d, e_ * FI * D, FIC, D, D)))
            return r

        extra = [cb_ms, sh_ms] if e_ == 0 else [cb_ms]
        lt, lv = gu_pairs(FIC, slab_f2, xn_rhs, O_ACTF, cslot, extra)
        w2f_ms = w2f_holder[0]
        dslot[2].guard = [lt]
        dslot[3].guard = [lt]
        vv = w2_stage(FIC, O_W2F, O_ACTF, [w2f_ms, lv])
        w2f_sl.guard = [t_ms()]
    accF_ms = v_ms()
    dump_acc("accF", accF_ms)

    XF0 = 20992
    x_fill = [2, 2]
    for kc in range(DC):
        buf = kc % 2
        xf = arf(XF0 + buf * 1024, XF0 + buf * 1024 + 1024)
        w = [v_ms()] if kc >= 2 else [sh_end_t]
        SY(lambda e, kc=kc, xf=xf: e.dma_start(xf, xin[kc * 128:(kc + 1) * 128, 0:1024]),
           waits=w, inc=(s_x[buf], 16))
        x_fill[buf] += 1
        xf_ms = (s_x[buf], 16 * x_fill[buf])
        ost = arf(buf * 1024, buf * 1024 + 1024)
        V(lambda e, kc=kc, xf=xf, ost=ost: e.tensor_tensor(
            ost, accf(kc * 1024, (kc + 1) * 1024), xf, ALU.add),
          waits=[xf_ms, accF_ms], inc=(s_v, 1))
        om = (s_v, s_v.count)
        SY(lambda e, kc=kc, ost=ost: e.dma_start(out[kc * 128:(kc + 1) * 128, :], ost),
           waits=[om], inc=(s_out, 16))
    fin = (s_out, s_out.count)
    V(lambda e: e.tensor_copy(STT[0:1, 40:41], STT[0:1, 40:41]), waits=[fin])

    with nc.Block() as block:
        P.replay(block)
    es.close()
    return nc



def _relayout(inputs):
    colsel = np.concatenate([np.arange(0, 128, 2), np.arange(1, 128, 2)])
    rotsel = np.concatenate([colsel[64:], colsel[:64]])
    wq_p = np.asarray(inputs["L0_attn_w_q"]).reshape(D, NH, HD)
    wq = np.ascontiguousarray(
        np.stack([wq_p[:, :, colsel], wq_p[:, :, rotsel]], axis=2).reshape(D, 2 * D))
    wk_p = np.asarray(inputs["L0_attn_w_k"]).reshape(D, NKV, HD)
    wk = np.ascontiguousarray(
        np.stack([wk_p[:, :, colsel], wk_p[:, :, rotsel]], axis=2).reshape(D, 2 * NKV * HD))
    wv = np.ascontiguousarray(np.asarray(inputs["L0_attn_w_v"]))
    wo = np.ascontiguousarray(np.asarray(inputs["L0_attn_w_o"]))
    anw = np.ascontiguousarray(np.asarray(inputs["L0_attn_attn_norm_w"]).reshape(DC, 128).T)
    mnw = np.ascontiguousarray(np.asarray(inputs["L0_attn_moe_norm_w"]).reshape(DC, 128).T)
    gwa = np.ascontiguousarray(np.asarray(inputs["L0_attn_gate_w"]))
    w1a = np.asarray(inputs["L0_attn_w1"]).reshape(AE, D, AIC, 128)
    w3a = np.asarray(inputs["L0_attn_w3"]).reshape(AE, D, AIC, 128)
    w13a = np.ascontiguousarray(np.stack([w1a, w3a], axis=3).reshape(AE, D, 2 * AI))
    w2a = np.ascontiguousarray(np.asarray(inputs["L0_attn_w2"]))
    gwf = np.ascontiguousarray(np.asarray(inputs["L0_ffn_gate_w"]).T)
    w1f_t = np.asarray(inputs["L0_ffn_w1"]).transpose(0, 2, 1).reshape(FE, D, 2, FIC, 128)
    w1f = np.ascontiguousarray(w1f_t.transpose(0, 1, 3, 2, 4)).reshape(FE, D, 2 * FI)
    w2f = np.ascontiguousarray(np.asarray(inputs["L0_ffn_w2"]).transpose(0, 2, 1))
    sw1_t = np.asarray(inputs["L0_ffn_shared_w1"]).T.reshape(D, 2, SIC, 128)
    sw1 = np.ascontiguousarray(sw1_t.transpose(0, 2, 1, 3)).reshape(D, 2 * SI)
    sw2 = np.ascontiguousarray(np.asarray(inputs["L0_ffn_shared_w2"]).T)
    return dict(wq=wq, wk=wk, wv=wv, wo=wo, anw=anw, mnw=mnw, gwa=gwa, w13a=w13a,
                w2a=w2a, gwf=gwf, w1f=w1f, w2f=w2f, sw1=sw1, sw2=sw2)


def _percore(x, core):
    seq, par = core // 2, core % 2
    blocks = np.concatenate([np.arange(par, 16, 2), np.arange(1 - par, 16, 2)])
    colpos = (blocks[:, None] * 128 + np.arange(128)[None, :]).reshape(-1)
    xin = np.ascontiguousarray(np.asarray(x[seq])[colpos, :].T)
    posr = np.ascontiguousarray(colpos.astype(np.float32)[None, :])
    pm = (np.zeros((128, 128), np.float32) if par == 1
          else np.full((128, 128), np.float32(NEG)))
    return xin, posr, pm, colpos


_NC_CACHE = {}


def kernel(**inputs):
    if "main" not in _NC_CACHE:
        _NC_CACHE["main"] = build_program()
    nc = _NC_CACHE["main"]
    shared = _relayout(inputs)
    x = np.asarray(inputs["L0_attn_x"])
    in_maps, cols = [], []
    for c in range(8):
        xi, posr, pm, colpos = _percore(x, c)
        m = dict(shared)
        m.update(xin=xi, posr=posr, pmask=pm)
        in_maps.append(m)
        cols.append(colpos)
    res = run_bass_kernel_spmd(nc, in_maps, core_ids=list(range(8)))
    outp = np.empty((4, L, D), np.float32)
    for c in range(8):
        o = res.results[c]["out"]
        outp[c // 2, cols[c][:1024], :] = np.asarray(o).T
    return outp
